# version 03 (final)
# baseline (speedup 1.0000x reference)
import numpy as np
import ml_dtypes
import concourse.bass as bass
import concourse.mybir as mybir
from concourse.bass import IndirectOffsetOnAxis
from bass_rust import ReduceOp
from concourse.bass_utils import run_bass_kernel_spmd
from contextlib import ExitStack

F32 = mybir.dt.float32
BF16 = mybir.dt.bfloat16
I32 = mybir.dt.int32
U32 = mybir.dt.uint32
A = mybir.AluOpType
ACT = mybir.ActivationFunctionType
AX = mybir.AxisListType

P = 128
D = 2048
NF = D // P
NCORE = 8
B, SEQL = 2, 8192
T = 2048
TSW = 512
NTS = T // TSW
NH, HD = 16, 128
RATIO, TOPK, WIN = 4, 8, 128
CSEQ = SEQL // RATIO
E, MOEH = 8, 2048
CAP = T if False else 2048
NSLOT = E * CAP
MLP = 16384
LN2 = float(np.log(2.0))
INV_LN2 = float(1.0 / np.log(2.0))
NEG = -1e30
ATTN_SCALE = float(HD ** -0.5)
LIN_SCALE = float(HD ** -0.5)

ENGS = ["sync", "tensor", "vector", "scalar", "gpsimd"]


class EngWrap:

    def __init__(self, eng, dsem, dcount):
        self._eng = eng
        self._dsem = dsem
        self._dcount = dcount
        self._last = None

    def dma_start(self, **kw):
        r = self._eng.dma_start(**kw)
        r.then_inc(self._dsem, 16)
        self._dcount[0] += 1
        return r

    def indirect_dma_start(self, **kw):
        r = self._eng.indirect_dma_start(**kw)
        r.then_inc(self._dsem, 16)
        self._dcount[0] += 1
        self._last = r
        return r

    def __getattr__(self, name):
        f = getattr(self._eng, name)
        if not callable(name) and not callable(f):
            return f

        def wrapped(*a, **kw):
            r = f(*a, **kw)
            if hasattr(r, "then_inc"):
                self._last = r
            return r
        return wrapped


class Sched:

    def __init__(self, nc, es):
        self.nc = nc
        self.stages = []
        self.sems = {E: es.enter_context(nc.semaphore(f"bar_{E}")) for E in ENGS}
        self.dsems = {E: es.enter_context(nc.semaphore(f"dma_{E}")) for E in ENGS}

    def stage(self):
        st = {E: [] for E in ENGS}
        self.stages.append(st)
        return st

    def emit(self, block, max_stages=None):
        if max_stages is not None:
            self.stages = self.stages[:max_stages]
        cum = {E: [0] for E in ENGS}
        for st in self.stages:
            for E in ENGS:
                cum[E].append(cum[E][-1] + (1 if st[E] else 0))
        for EN in ENGS:
            def make(EN):
                def body(eng):
                    dcount = [0]
                    wrap = EngWrap(eng, self.dsems[EN], dcount)
                    for si, st in enumerate(self.stages):
                        last = si == len(self.stages) - 1
                        if not st[EN] and not last:
                            continue
                        for E2 in ENGS:
                            if E2 != EN and cum[E2][si] > 0:
                                eng.wait_ge(self.sems[E2], cum[E2][si])
                        before = dcount[0]
                        wrap._last = None
                        for fn in st[EN]:
                            fn(wrap)
                        if dcount[0] > before:
                            eng.wait_ge(self.dsems[EN], 16 * dcount[0])
                        if st[EN]:
                            if dcount[0] > before:
                                eng.sem_inc(self.sems[EN], 1)
                            elif wrap._last is not None:
                                wrap._last.then_inc(self.sems[EN], 1)
                            else:
                                eng.sem_inc(self.sems[EN], 1)
                return body
            getattr(block, EN)(make(EN))


class ConstPool:

    def __init__(self, nc, es, sch, name="cpool", n=16):
        self.dram = nc.dram_tensor(name, [P, n], F32, kind="ExternalInput").ap()
        self.tile = es.enter_context(nc.sbuf_tensor(name + "_sb", [P, n], F32))
        self.vals = {}
        self.n = n
        st = sch.stage()
        st["sync"].append(lambda e: e.dma_start(out=self.tile[:], in_=self.dram))
        nc._cpool = self

    def col(self, val, parts=P):
        val = float(val)
        if val not in self.vals:
            assert len(self.vals) < self.n
            self.vals[val] = len(self.vals)
        i = self.vals[val]
        return self.tile[0:parts, i:i + 1]

    def host_array(self):
        a = np.zeros((P, self.n), np.float32)
        for v, i in self.vals.items():
            a[:, i] = v
        return a


def _rearr_fm(dram_ap, p=P):
    return dram_ap.rearrange("(c p) t -> p c t", p=p)


def _sb3(sbuf_ap, w):
    return sbuf_ap.rearrange("p (c t) -> p c t", t=w)


def ln_emit(sch, xs, h0s, tmp, psA, psB, onesM, w, bca, bcb, cp,
            eps=1e-5, nchunks=NF, nfeat=D):
    st = sch.stage()
    import os as _os
    _sqb = _os.environ.get("SQB", "ap")
    for c in range(nchunks):
        st["scalar"].append(lambda e, c=c: e.activation(
            out=tmp[:, c * w:(c + 1) * w], in_=xs[:, c * w:(c + 1) * w],
            func=ACT.Square, bias=(0.0 if _sqb == "f" else cp.col(0.0)), scale=1.0))
    st = sch.stage()
    def mms(e):
        for c in range(nchunks):
            e.matmul(psA[:, 0:w], onesM[:], xs[:, c * w:(c + 1) * w],
                     start=(c == 0), stop=(c == nchunks - 1))
        for c in range(nchunks):
            e.matmul(psB[:, 0:w], onesM[:], tmp[:, c * w:(c + 1) * w],
                     start=(c == 0), stop=(c == nchunks - 1))
    st["tensor"].append(mms)
    st = sch.stage()
    import os
    _nrow = int(os.environ.get("LNROWS", "4"))
    def rows(e):
        if _nrow >= 1:
            e.tensor_scalar(out=bca[:], in0=psA[:, 0:w], scalar1=1.0 / nfeat, scalar2=None,
                            op0=A.mult)
        if _nrow >= 2:
            e.tensor_scalar(out=bcb[:], in0=psB[:, 0:w], scalar1=1.0 / nfeat, scalar2=None,
                            op0=A.mult)
        if _nrow >= 3:
            e.tensor_tensor(out=tmp[:, 0:w], in0=bca[:], in1=bca[:], op=A.mult)
        if _nrow >= 4:
            e.tensor_tensor(out=bcb[:], in0=bcb[:], in1=tmp[:, 0:w], op=A.subtract)
    st["vector"].append(rows)
    st = sch.stage()
    def sqr(e):
        t0 = tmp[:, 0:w]
        t1 = tmp[:, w:2 * w]
        e.activation(out=t0, in_=bcb[:], func=ACT.Sqrt, bias=cp.col(eps),
                     scale=1.0)
        e.activation(out=t1, in_=t0, func=ACT.Ln, bias=cp.col(0.0), scale=1.0)
        e.activation(out=bcb[:], in_=t1, func=ACT.Exp, bias=cp.col(0.0),
                     scale=-1.0)
    st["scalar"].append(sqr)
    st = sch.stage()
    def rows2(e):
        e.tensor_tensor(out=bca[:], in0=bca[:], in1=bcb[:], op=A.mult)
        e.tensor_scalar(out=bca[:], in0=bca[:], scalar1=-1.0, scalar2=None,
                        op0=A.mult)
    st["vector"].append(rows2)
    st = sch.stage()
    def apply(e):
        for c in range(nchunks):
            dst = h0s[:, c * w:(c + 1) * w]
            e.tensor_tensor(out=dst, in0=xs[:, c * w:(c + 1) * w],
                            in1=bcb[:], op=A.mult)
            e.tensor_tensor(out=dst, in0=dst, in1=bca[:], op=A.add)
    st["vector"].append(apply)


def proj_emit(sch, nc, h_in, w, out_sb, dw, wbuf, bias_c, pb, cp, n_out_chunks=NF,
              first_prefetch=True, act=None, nk=NF, post=None):
    WB = nk * P
    for ec in range(n_out_chunks):
        st = sch.stage()
        st["sync"].append(lambda e, ec=ec: e.dma_start(
            out=_sb3(wbuf[:, 0:WB], P), in_=_rearr_fm(dw[:, ec * P:(ec + 1) * P])))
        st = sch.stage()
        def mm(e, ec=ec):
            for c in range(nk):
                e.matmul(pb[ec % 2][:], wbuf[:, c * P:(c + 1) * P],
                         h_in[:, c * w:(c + 1) * w],
                         start=(c == 0), stop=(c == nk - 1))
        st["tensor"].append(mm)
        st = sch.stage()
        def ev(e, ec=ec):
            dst = out_sb[:, ec * w:(ec + 1) * w]
            if act is not None:
                e.activation(out=dst, in_=pb[ec % 2][:], func=act,
                             bias=cp.col(0.0), scale=1.0)
            else:
                e.copy(out=dst, in_=pb[ec % 2][:])
        st["scalar"].append(ev)
    if bias_c is not None:
        st = sch.stage()
        def badd(e):
            for ecm in range(n_out_chunks):
                dst = out_sb[:, ecm * w:(ecm + 1) * w]
                e.tensor_scalar(out=dst, in0=dst, scalar1=bias_c[:, ecm:ecm + 1],
                                scalar2=None, op0=A.add)
        st["vector"].append(badd)


def phi_emit(sch, ks, tmp, tmp2, w, cp):
    st = sch.stage()
    for c in range(NF):
        st["vector"].append(lambda e, c=c: e.tensor_scalar(
            out=tmp[:, c * w:(c + 1) * w], in0=ks[:, c * w:(c + 1) * w],
            scalar1=0.0, scalar2=None, op0=A.min))
    st = sch.stage()
    for c in range(NF):
        st["scalar"].append(lambda e, c=c: e.activation(
            out=tmp2[:, c * w:(c + 1) * w], in_=tmp[:, c * w:(c + 1) * w],
            func=ACT.Exp, bias=cp.col(0.0), scale=1.0))
    st = sch.stage()
    def rest(e):
        for c in range(NF):
            K = ks[:, c * w:(c + 1) * w]
            LE = tmp[:, c * w:(c + 1) * w]
            EX = tmp2[:, c * w:(c + 1) * w]
            e.tensor_scalar(out=LE, in0=K, scalar1=0.0, scalar2=None, op0=A.is_le)
            e.tensor_tensor(out=EX, in0=EX, in1=LE, op=A.mult)
            e.tensor_scalar(out=K, in0=K, scalar1=1.0, scalar2=None, op0=A.add)
            e.tensor_tensor(out=LE, in0=K, in1=LE, op=A.mult)
            e.tensor_tensor(out=K, in0=K, in1=EX, op=A.add)
            e.tensor_tensor(out=K, in0=K, in1=LE, op=A.subtract)
    st["vector"].append(rest)


def build_launch1(max_stages=None):
    nc = bass.Bass("TRN2", target_bir_lowering=False, detect_race_conditions=False)
    d_x = nc.dram_tensor("x", [D, T], F32, kind="ExternalInput").ap()
    d_wkT = nc.dram_tensor("wkT", [D, D], F32, kind="ExternalInput").ap()
    d_wvT = nc.dram_tensor("wvT", [D, D], F32, kind="ExternalInput").ap()
    d_bk = nc.dram_tensor("bk", [P, NF], F32, kind="ExternalInput").ap()
    d_bv = nc.dram_tensor("bv", [P, NF], F32, kind="ExternalInput").ap()
    d_zcum = nc.dram_tensor("zcum", [D, T], F32, kind="ExternalOutput").ap()
    d_scar = nc.dram_tensor("scar", [P, NF], F32, kind="ExternalOutput").ap()
    import os as _os
    _dbg = _os.environ.get("L1DBG")
    if _dbg:
        d_dbg = {nm: nc.dram_tensor("dbg_" + nm, [P, NF * TSW], F32,
                                    kind="ExternalOutput").ap()
                 for nm in ("x", "h0", "k", "phik", "v")}

    es = ExitStack()
    with es:
        sch = Sched(nc, es)
        cp = ConstPool(nc, es, sch)
        def sb(name, shape, dt=F32):
            return es.enter_context(nc.sbuf_tensor(name, shape, dt))
        xs = sb("xs", [P, NF * TSW])
        ks = sb("ks", [P, NF * TSW])
        vs_ = sb("vs", [P, NF * TSW])
        zs = sb("zs", [P, NF * TSW])
        tmp = sb("tmp", [P, NF * TSW])
        tmp2 = zs
        wbuf = sb("wbuf", [P, 2 * NF * P])
        bkc = sb("bkc", [P, NF])
        bvc = sb("bvc", [P, NF])
        carry = sb("carry", [P, NF])
        onesM = sb("onesM", [P, P])
        onesw = sb("onesw", [P, TSW])
        bca = sb("bca", [P, TSW])
        bcb = sb("bcb", [P, TSW])
        psA = es.enter_context(nc.psum_tensor("psA", [P, TSW], F32))
        psB = es.enter_context(nc.psum_tensor("psB", [P, TSW], F32))
        pb = [es.enter_context(nc.psum_tensor(f"pb{i}", [P, TSW], F32)) for i in range(2)]
        block = es.enter_context(nc.Block())

        st = sch.stage()
        st["sync"].append(lambda e: e.dma_start(out=bkc[:], in_=d_bk))
        st["sync"].append(lambda e: e.dma_start(out=bvc[:], in_=d_bv))
        st["vector"].append(lambda e: e.memset(onesM[:], 1.0))
        st["vector"].append(lambda e: e.memset(onesw[:], 1.0))
        st["vector"].append(lambda e: e.memset(carry[:], 0.0))

        def dbg_dump(nm, src):
            if not _dbg:
                return
            st = sch.stage()
            st["gpsimd"].append(lambda e: e.dma_start(out=d_dbg[nm], in_=src[:]))

        for ts in range(NTS):
            st = sch.stage()
            import os as _os2
            if _os2.environ.get("NOX3D"):
                st["sync"].append(lambda e, ts=ts: e.dma_start(
                    out=xs[:], in_=d_x[0:P, 0:NF * TSW]))
            else:
                st["sync"].append(lambda e, ts=ts: e.dma_start(
                    out=_sb3(xs[:], TSW), in_=_rearr_fm(d_x[:, ts * TSW:(ts + 1) * TSW])))
            if ts == 0:
                dbg_dump("x", xs)
            ln_emit(sch, xs, xs, tmp, psA, psB, onesM, TSW, bca, bcb, cp)
            if ts == 0:
                dbg_dump("h0", xs)
            proj_emit(sch, nc, xs, TSW, ks, d_wkT, wbuf, bkc, pb, cp)
            proj_emit(sch, nc, xs, TSW, vs_, d_wvT, wbuf, bvc, pb, cp)
            if ts == 0:
                dbg_dump("k", ks)
                dbg_dump("v", vs_)
            phi_emit(sch, ks, tmp, tmp2, TSW, cp)
            if ts == 0:
                dbg_dump("phik", ks)
            st = sch.stage()
            def zmul(e):
                for c in range(NF):
                    e.tensor_tensor(out=vs_[:, c * TSW:(c + 1) * TSW],
                                    in0=vs_[:, c * TSW:(c + 1) * TSW],
                                    in1=ks[:, c * TSW:(c + 1) * TSW], op=A.mult)
            st["vector"].append(zmul)
            st = sch.stage()
            def scan(e):
                for c in range(NF):
                    e.tensor_tensor_scan(out=zs[:, c * TSW:(c + 1) * TSW],
                                         data0=onesw[:], data1=vs_[:, c * TSW:(c + 1) * TSW],
                                         initial=carry[:, c:c + 1],
                                         op0=A.mult, op1=A.add)
            st["vector"].append(scan)
            st = sch.stage()
            def upcar(e):
                for c in range(NF):
                    e.tensor_copy(out=carry[:, c:c + 1],
                                  in_=zs[:, (c + 1) * TSW - 1:(c + 1) * TSW])
            st["vector"].append(upcar)
            st["gpsimd"].append(lambda e, ts=ts: e.dma_start(
                out=_rearr_fm(d_zcum[:, ts * TSW:(ts + 1) * TSW]), in_=_sb3(zs[:], TSW)))
        st = sch.stage()
        st["gpsimd"].append(lambda e: e.dma_start(out=d_scar, in_=carry[:]))
        sch.emit(block, max_stages)
    return nc


def _fmT(a):
    return np.ascontiguousarray(np.asarray(a).T.astype(np.float32))


def _bias_cols(b):
    return np.ascontiguousarray(np.asarray(b, np.float32).reshape(NF, P).T)





def act512(e, out, in_, func, bias, scale, w):
    for j0 in range(0, w, 512):
        e.activation(out=out[:, j0:j0 + 512], in_=in_[:, j0:j0 + 512],
                     func=func, bias=bias, scale=scale)


def _floor_ladder(e, t, out, iscr, d):
    e.tensor_copy(out=iscr, in_=t)
    e.tensor_copy(out=out, in_=iscr)
    e.tensor_tensor(out=d, in0=t, in1=out, op=A.subtract)
    e.scalar_tensor_tensor(out=out, in0=d, scalar=1.0, in1=out,
                           op0=A.is_ge, op1=A.add)
    e.tensor_scalar(out=d, in0=d, scalar1=0.0, scalar2=None, op0=A.is_lt)
    e.tensor_tensor(out=out, in0=out, in1=d, op=A.subtract)


def _rint_ladder(e, r, out, iscr, d):
    e.tensor_copy(out=iscr, in_=r)
    e.tensor_copy(out=out, in_=iscr)
    e.tensor_tensor(out=d, in0=r, in1=out, op=A.subtract)
    e.scalar_tensor_tensor(out=out, in0=d, scalar=0.5, in1=out,
                           op0=A.is_gt, op1=A.add)
    e.tensor_scalar(out=d, in0=d, scalar1=-0.5, scalar2=None, op0=A.is_lt)
    e.tensor_tensor(out=out, in0=out, in1=d, op=A.subtract)


def _pow2_bits(e, u, out, b, t, pairs):
    first = True
    for (bv, hi, lo) in pairs:
        e.tensor_scalar(out=b, in0=u, scalar1=float(bv), scalar2=None,
                        op0=A.is_ge)
        e.scalar_tensor_tensor(out=t, in0=b, scalar=float(-bv), in1=u,
                               op0=A.mult, op1=A.add)
        e.tensor_copy(out=u, in_=t)
        e.tensor_scalar(out=b, in0=b, scalar1=float(hi - lo), scalar2=None,
                        op0=A.mult)
        e.tensor_scalar(out=b, in0=b, scalar1=float(lo), scalar2=None,
                        op0=A.add)
        if first:
            e.tensor_copy(out=out, in_=b)
            first = False
        else:
            e.tensor_tensor(out=out, in0=out, in1=b, op=A.mult)


_P2_SCALE = [(128, 2.0**64, 2.0**-64), (64, 2.0**32, 2.0**-32),
             (32, 2.0**16, 2.0**-16), (16, 2.0**8, 2.0**-8),
             (8, 2.0**4, 2.0**-4), (4, 2.0**2, 2.0**-2),
             (2, 2.0**1, 2.0**-1), (1, 2.0**1, 2.0**0)]
_P2_STEP = [(8, 2.0**6, 2.0**-2), (4, 2.0**2, 2.0**-2),
            (2, 2.0**1, 2.0**-1), (1, 2.0**1, 2.0**0)]


def qdq_emit(sch, x, outbf, scr, w, cp):
    nb = w // 32
    x3 = x.rearrange("p (b k) -> p b k", k=32)
    a, b2, ints = scr["a"], scr["b"], scr["ints"]
    g1, g2, g3, g4, gi = scr["g1"], scr["g2"], scr["g3"], scr["g4"], scr["gi"]
    a3 = a.rearrange("p (b k) -> p b k", k=32)
    b3 = b2.rearrange("p (b k) -> p b k", k=32)

    st = sch.stage()
    def grid1(e):
        e.memset(g1, 1.0)
        e.tensor_reduce(out=g1[:, 0:nb], in_=x3, axis=AX.X, op=A.max,
                        apply_absolute_value=True)
        e.tensor_scalar(out=g2, in0=g1, scalar1=0.0, scalar2=None, op0=A.is_le)
        e.tensor_tensor(out=g1, in0=g1, in1=g2, op=A.add)
    st["vector"].append(grid1)
    st = sch.stage()
    st["scalar"].append(lambda e: e.activation(out=g2, in_=g1, func=ACT.Ln,
                                               bias=cp.col(0.0), scale=1.0))
    st = sch.stage()
    def grid2(e):
        e.tensor_scalar(out=g2, in0=g2, scalar1=INV_LN2, scalar2=None,
                        op0=A.mult)
        _floor_ladder(e, g2, g1, gi, g3)
        e.tensor_scalar(out=g1, in0=g1, scalar1=-1.0, scalar2=None, op0=A.mult)
        e.tensor_scalar(out=g1, in0=g1, scalar1=8.0, scalar2=None, op0=A.add)
        e.tensor_scalar(out=g1, in0=g1, scalar1=127.0, scalar2=None, op0=A.min)
        e.tensor_scalar(out=g1, in0=g1, scalar1=-127.0, scalar2=None, op0=A.max)
        e.tensor_scalar(out=g1, in0=g1, scalar1=127.0, scalar2=None, op0=A.add)
        _pow2_bits(e, g1, g2, g3, g4, _P2_SCALE)
    st["vector"].append(grid2)
    st = sch.stage()
    def grid3(e):
        e.activation(out=g3, in_=g2, func=ACT.Ln, bias=cp.col(0.0), scale=1.0)
        e.activation(out=g4, in_=g3, func=ACT.Exp, bias=cp.col(0.0), scale=-1.0)
    st["scalar"].append(grid3)
    st = sch.stage()
    def elem1(e):
        e.tensor_tensor(out=a3, in0=x3,
                        in1=g2[:, 0:nb].broadcast_to([P, nb, 32]),
                        op=A.mult)
        e.tensor_scalar(out=a, in0=a, scalar1=448.0, scalar2=None, op0=A.min)
        e.tensor_scalar(out=a, in0=a, scalar1=-448.0, scalar2=None, op0=A.max)
        e.tensor_scalar(out=b2, in0=a, scalar1=-1.0, scalar2=None, op0=A.mult)
        e.tensor_tensor(out=b2, in0=a, in1=b2, op=A.max)
        e.tensor_scalar(out=b2, in0=b2, scalar1=1e-30, scalar2=None,
                        op0=A.max)
    st["vector"].append(elem1)
    st = sch.stage()
    st["scalar"].append(lambda e: act512(e, b2, b2, ACT.Ln, cp.col(0.0), 1.0, w))
    st = sch.stage()
    def elem2(e):
        e.tensor_scalar(out=b2, in0=b2, scalar1=INV_LN2, scalar2=None,
                        op0=A.mult)
        _floor_ladder(e, b2, scr["c"], ints, b2)
        e.tensor_scalar(out=scr["c"], in0=scr["c"], scalar1=-6.0, scalar2=None,
                        op0=A.max)
        e.tensor_scalar(out=scr["c"], in0=scr["c"], scalar1=-1.0, scalar2=None,
                        op0=A.mult)
        e.tensor_scalar(out=scr["c"], in0=scr["c"], scalar1=8.0, scalar2=None,
                        op0=A.add)
        _pow2_bits(e, scr["c"], b2, scr["d"], scr["e2"], _P2_STEP)
        e.tensor_tensor(out=scr["c"], in0=a, in1=b2, op=A.mult)
        _rint_ladder(e, scr["c"], scr["d"], ints, scr["e2"])
    st["vector"].append(elem2)
    st = sch.stage()
    def stp(e):
        act512(e, scr["e2"], b2, ACT.Ln, cp.col(0.0), 1.0, w)
        act512(e, b2, scr["e2"], ACT.Exp, cp.col(0.0), -1.0, w)
    st["scalar"].append(stp)
    st = sch.stage()
    def elem3(e):
        e.tensor_tensor(out=a, in0=scr["d"], in1=b2, op=A.mult)
        e.tensor_tensor(out=a3, in0=a3,
                        in1=g4[:, 0:nb].broadcast_to([P, nb, 32]),
                        op=A.mult)
        e.tensor_copy(out=outbf, in_=a)
    st["vector"].append(elem3)


def build_launch2():
    nc = bass.Bass("TRN2", target_bir_lowering=False, detect_race_conditions=False)
    din = {}
    for nm, shp in [("x", [D, T]), ("zcum", [D, T]), ("carries", [P, 3 * NF]),
                    ("wqT", [D, D]), ("bq", [P, NF]),
                    ("woT", [D, D]), ("bo", [P, NF]),
                    ("gwT", [D, 8]), ("ebias", [P, 8]),
                    ("sw1T", [D, D]), ("sw3T", [D, D]), ("sw2T", [D, D]),
                    ("w13", [E * 2 * MOEH, D]), ("w2", [E * D, MOEH]),
                    ("wk1", [D, HD]), ("bk1", [P, 1]),
                    ("wv1", [D, HD]), ("bv1", [P, 1]),
                    ("wcmp", [D, P]), ("bcmp", [P, 1]),
                    ("iden", [P, P]), ("iota32", [32, T]), ("tril32", [32, 32]),
                    ("bas32", [32, 2])]:
        din[nm] = nc.dram_tensor(nm, shp, F32, kind="ExternalInput").ap()
    dout = {}
    for nm, shp in [("x2", [D, T]), ("kraw", [HD, T]), ("vraw", [HD, T]),
                    ("cmplog", [1, T])]:
        dout[nm] = nc.dram_tensor(nm, shp, F32, kind="ExternalOutput").ap()
    import os as _os
    _k = "ExternalOutput" if _os.environ.get("L2DBG") else "Internal"
    d_h1tm = nc.dram_tensor("h1tm", [T, D], F32, kind=_k).ap()
    d_x2p = nc.dram_tensor("x2p", [D, T], F32, kind=_k).ap()
    d_xg = nc.dram_tensor("xg", [NSLOT + E * T, D], F32, kind=_k).ap()
    d_rout = nc.dram_tensor("rout", [NSLOT + E * T, D], F32, kind=_k).ap()
    d_w13qT = nc.dram_tensor("w13qT", [E * D, 2 * MOEH], BF16, kind=_k).ap()
    d_w2qT = nc.dram_tensor("w2qT", [E * MOEH, D], BF16, kind="Internal").ap()

    es = ExitStack()
    with es:
        sch = Sched(nc, es)
        cp = ConstPool(nc, es, sch)
        def sb(name, shape, dt=F32):
            return es.enter_context(nc.sbuf_tensor(name, shape, dt))
        NW = NF * TSW
        Abuf = sb("Abuf", [P, NW]); Bbuf = sb("Bbuf", [P, NW])
        Cbuf = sb("Cbuf", [P, NW]); Dbuf = sb("Dbuf", [P, NW])
        Rbuf = sb("Rbuf", [P, NW])
        ints = sb("ints", [P, 2048], I32)
        wbuf = sb("wbuf", [P, NF * P])
        wq16 = sb("wq16", [P, NF * TSW], BF16)
        bca = sb("bca", [P, TSW]); bcb = sb("bcb", [P, TSW])
        onesM = sb("onesM", [P, P])
        carry = sb("carry", [P, NF])
        bqc = sb("bqc", [P, NF]); boc = sb("boc", [P, NF])
        gws = sb("gws", [P, NF * 8])
        logit = sb("logit", [P, 16 * 8])
        g12 = sb("g12", [P, 2 * 16])
        slotc = sb("slotc", [P, 2 * 16])
        slotu = sb("slotu", [P, 2 * 16], U32)
        idn = sb("idn", [P, P]); idnb = sb("idnb", [P, P], BF16)
        ebc = sb("ebc", [P, 8])
        tril32 = sb("tril32_sb", [32, 32]); ones32M = sb("ones32M", [32, P])
        kb1c = sb("kb1c", [P, 1]); vb1c = sb("vb1c", [P, 1]); cb1c = sb("cb1c", [P, 1])
        bas32 = sb("bas32_sb", [32, 2])
        xqT16 = sb("xqT16", [P, 2048], BF16)
        pb = [es.enter_context(nc.psum_tensor(f"pb{i}", [P, TSW], F32)) for i in range(2)]
        psA = es.enter_context(nc.psum_tensor("psA", [P, TSW], F32))
        psB = es.enter_context(nc.psum_tensor("psB", [P, TSW], F32))
        psT = es.enter_context(nc.psum_tensor("psT", [P, TSW], F32))
        psb8 = es.enter_context(nc.psum_tensor("psb8", [P, 8], F32))
        pstb = es.enter_context(nc.psum_tensor("pstb", [P, 1024], BF16))
        block = es.enter_context(nc.Block())

        mm16 = wq16[:, 4096:6144]
        qscr = {"a": Rbuf[:, 2048:4096], "b": Rbuf[:, 4096:6144],
                "c": Cbuf[:, 4096:6144], "d": Dbuf[:, 4096:6144],
                "e2": Dbuf[:, 6144:8192], "ints": ints[:],
                "g1": Cbuf[:, 6144:6656], "g2": Cbuf[:, 6656:7168],
                "g3": Cbuf[:, 7168:7680], "g4": Cbuf[:, 7680:8192],
                "gi": ints[:, 0:512]}

        st = sch.stage()
        for t_, src in [(carry, din["carries"][:, 0:NF]), (bqc, din["bq"]),
                        (boc, din["bo"]), (ebc, din["ebias"]),
                        (tril32, din["tril32"]),
                        (idn, din["iden"]), (bas32, din["bas32"]),
                        (kb1c, din["bk1"]), (vb1c, din["bv1"]),
                        (cb1c, din["bcmp"])]:
            st["sync"].append(lambda e, t_=t_, src=src: e.dma_start(out=t_[:], in_=src))
        st["sync"].append(lambda e: e.dma_start(out=_sb3(gws[:], 8),
                                                in_=_rearr_fm(din["gwT"])))
        st["vector"].append(lambda e: e.memset(onesM[:], 1.0))
        st["vector"].append(lambda e: e.memset(ones32M[:], 1.0))
        st["sync"].append(lambda e: e.dma_start(out=Rbuf[:, 0:NF],
                                                in_=din["carries"][:, NF:2 * NF]))
        st["sync"].append(lambda e: e.dma_start(out=Rbuf[:, NF:2 * NF],
                                                in_=din["carries"][:, 2 * NF:3 * NF]))
        st = sch.stage()
        def addc(e):
            e.tensor_tensor(out=Rbuf[:, 2 * NF:3 * NF], in0=Rbuf[:, 0:NF],
                            in1=Rbuf[:, NF:2 * NF], op=A.add)
            e.tensor_copy(out=idnb[:], in_=idn[:])
            e.tensor_copy(out=Rbuf[:, 3 * NF:3 * NF + P], in_=idn[:, 0:P])
            e.tensor_tensor(out=carry[:], in0=carry[:],
                            in1=Rbuf[:, 2 * NF:3 * NF], op=A.add)
        st["vector"].append(addc)

        def wprep(dsrc, nrows, ddstT, dst_ncols):
            for n0 in range(0, nrows, P):
                st = sch.stage()
                st["sync"].append(lambda e, n0=n0: e.dma_start(
                    out=Rbuf[:, 0:2048], in_=dsrc[n0:n0 + P, :]))
                st = sch.stage()
                st["vector"].append(lambda e: e.tensor_copy(
                    out=Abuf[:, 0:2048], in_=Rbuf[:, 0:2048]))
                qdq_emit(sch, Abuf[:, 0:2048], mm16, qscr, 2048, cp)
                for grp in range(4):
                    st = sch.stage()
                    def trg(e, grp=grp):
                        for j in range(4):
                            c = grp * 4 + j
                            e.transpose(pstb[:, j * P:(j + 1) * P],
                                        wq16[:, 4096 + c * P:4096 + (c + 1) * P], idnb[:])
                    st["tensor"].append(trg)
                    st = sch.stage()
                    st["vector"].append(lambda e, grp=grp: e.tensor_copy(
                        out=wq16[:, grp * TSW:grp * TSW + TSW],
                        in_=pstb[:, 0:TSW]))
                st = sch.stage()
                st["gpsimd"].append(lambda e, n0=n0: e.dma_start(
                    out=ddstT[:, n0:n0 + P].rearrange("(c p) t -> p c t", p=P),
                    in_=_sb3(wq16[:, 0:NF * P], P)))
        for e_ in range(E):
            wprep(din["w13"][e_ * 2 * MOEH:(e_ + 1) * 2 * MOEH, :], 2 * MOEH,
                  d_w13qT[e_ * D:(e_ + 1) * D, :], 2 * MOEH)
        for e_ in range(E):
            wprep(din["w2"][e_ * D:(e_ + 1) * D, :], D,
                  d_w2qT[e_ * MOEH:(e_ + 1) * MOEH, :], D)

        for ts in range(NTS):
            sl = slice(ts * TSW, (ts + 1) * TSW)
            st = sch.stage()
            st["sync"].append(lambda e, sl=sl: e.dma_start(
                out=_sb3(Abuf[:], TSW), in_=_rearr_fm(din["x"][:, sl])))
            ln_emit(sch, Abuf, Bbuf, Cbuf, psA, psB, onesM, TSW, bca, bcb, cp)
            proj_emit(sch, nc, Bbuf, TSW, Cbuf, din["wqT"], wbuf, bqc, pb, cp)
            phi_emit(sch, Cbuf, Bbuf, Dbuf, TSW, cp)
            st = sch.stage()
            st["sync"].append(lambda e, sl=sl: e.dma_start(
                out=_sb3(Dbuf[:], TSW), in_=_rearr_fm(din["zcum"][:, sl])))
            st = sch.stage()
            def ctxf(e):
                for c in range(NF):
                    e.tensor_scalar(out=Dbuf[:, c * TSW:(c + 1) * TSW],
                                    in0=Dbuf[:, c * TSW:(c + 1) * TSW],
                                    scalar1=carry[:, c:c + 1], scalar2=LIN_SCALE,
                                    op0=A.add, op1=A.mult)
                    e.tensor_tensor(out=Dbuf[:, c * TSW:(c + 1) * TSW],
                                    in0=Dbuf[:, c * TSW:(c + 1) * TSW],
                                    in1=Cbuf[:, c * TSW:(c + 1) * TSW], op=A.mult)
            st["vector"].append(ctxf)
            proj_emit(sch, nc, Dbuf, TSW, Cbuf, din["woT"], wbuf, boc, pb, cp)
            st = sch.stage()
            def resid(e):
                for c in range(NF):
                    e.tensor_tensor(out=Abuf[:, c * TSW:(c + 1) * TSW],
                                    in0=Abuf[:, c * TSW:(c + 1) * TSW],
                                    in1=Cbuf[:, c * TSW:(c + 1) * TSW], op=A.add)
            st["vector"].append(resid)
            ln_emit(sch, Abuf, Bbuf, Cbuf, psA, psB, onesM, TSW, bca, bcb, cp)
            for sub in range(4):
                for grp in range(4):
                    st = sch.stage()
                    def trg(e, sub=sub, grp=grp):
                        for j in range(4):
                            c = grp * 4 + j
                            e.transpose(psT[:, j * P:(j + 1) * P],
                                        Bbuf[:, c * TSW + sub * P:c * TSW + (sub + 1) * P],
                                        idn[:])
                    st["tensor"].append(trg)
                    st = sch.stage()
                    st["vector"].append(lambda e, grp=grp: e.tensor_copy(
                        out=Cbuf[:, grp * TSW:(grp + 1) * TSW], in_=psT[:]))
                st = sch.stage()
                st["gpsimd"].append(lambda e, ts=ts, sub=sub: e.dma_start(
                    out=d_h1tm[(ts * 4 + sub) * P:(ts * 4 + sub + 1) * P, :],
                    in_=Cbuf[:, 0:4 * TSW]))
            for sub in range(4):
                st = sch.stage()
                def gmm(e, sub=sub):
                    for c in range(NF):
                        hsl = slice(c * TSW + sub * P, c * TSW + (sub + 1) * P)
                        e.matmul(psb8[:], Bbuf[:, hsl], gws[:, c * 8:(c + 1) * 8],
                                 start=(c == 0), stop=(c == NF - 1))
                st["tensor"].append(gmm)
                st = sch.stage()
                st["vector"].append(lambda e, ts=ts, sub=sub: e.tensor_copy(
                    out=logit[:, (ts * 4 + sub) * 8:(ts * 4 + sub + 1) * 8],
                    in_=psb8[:]))
            proj_emit(sch, nc, Bbuf, TSW, Cbuf, din["sw1T"], wbuf, None, pb, cp,
                      act=ACT.Silu)
            proj_emit(sch, nc, Bbuf, TSW, Dbuf, din["sw3T"], wbuf, None, pb, cp)
            st = sch.stage()
            def silmul(e):
                for c in range(NF):
                    e.tensor_tensor(out=Cbuf[:, c * TSW:(c + 1) * TSW],
                                    in0=Cbuf[:, c * TSW:(c + 1) * TSW],
                                    in1=Dbuf[:, c * TSW:(c + 1) * TSW], op=A.mult)
            st["vector"].append(silmul)
            proj_emit(sch, nc, Cbuf, TSW, Dbuf, din["sw2T"], wbuf, None, pb, cp)
            st = sch.stage()
            def x2pf(e):
                for c in range(NF):
                    e.tensor_tensor(out=Abuf[:, c * TSW:(c + 1) * TSW],
                                    in0=Abuf[:, c * TSW:(c + 1) * TSW],
                                    in1=Dbuf[:, c * TSW:(c + 1) * TSW], op=A.add)
            st["vector"].append(x2pf)
            st = sch.stage()
            st["gpsimd"].append(lambda e, sl=sl: e.dma_start(
                out=_rearr_fm(d_x2p[:, sl]), in_=_sb3(Abuf[:], TSW)))

        r32 = Rbuf[0:32, 0:T]
        grows = Rbuf[0:32, 2048:2048 + T]
        pos32 = Rbuf[0:32, 4096:4096 + T]
        ones32 = Rbuf[0:32, 6144:6144 + T]
        rnk = Cbuf[0:32, 0:T]
        s12 = Dbuf[0:32, 0:4 * T]
        gk = Cbuf[0:32, 2048:2048 + 2 * T]
        redbuf = Bbuf[:, 0:4 * T]
        srows = Cbuf[0:32, 6144:6144 + T]
        st = sch.stage()
        st["scalar"].append(lambda e: e.activation(
            out=Abuf[:, 0:128], in_=logit[:], func=ACT.Sigmoid,
            bias=cp.col(0.0), scale=1.0))
        st = sch.stage()
        def cho2(e):
            e.memset(Rbuf[:, 0:NW], 0.0)
            for sub in range(16):
                e.tensor_tensor(out=Abuf[:, 128 + sub * 8:128 + (sub + 1) * 8],
                                in0=Abuf[:, sub * 8:(sub + 1) * 8],
                                in1=ebc[:], op=A.add)
        st["vector"].append(cho2)
        st = sch.stage()
        def srt(e):
            for sub in range(16):
                e.max(out=Abuf[:, 256 + sub * 8:256 + (sub + 1) * 8],
                      in_=Abuf[:, 128 + sub * 8:128 + (sub + 1) * 8])
        st["vector"].append(srt)
        st = sch.stage()
        def msk(e):
            for sub in range(16):
                e.tensor_scalar(out=Abuf[:, 384 + sub * 8:384 + (sub + 1) * 8],
                                in0=Abuf[:, 128 + sub * 8:128 + (sub + 1) * 8],
                                scalar1=Abuf[:, 256 + sub * 8 + 1:256 + sub * 8 + 2],
                                scalar2=None, op0=A.is_ge)
            for sub in range(16):
                e.tensor_tensor(out=Abuf[:, 512 + sub * 8:512 + (sub + 1) * 8],
                                in0=Abuf[:, 384 + sub * 8:384 + (sub + 1) * 8],
                                in1=Abuf[:, sub * 8:(sub + 1) * 8], op=A.mult)
            for sub in range(16):
                e.tensor_reduce(out=Abuf[:, 640 + sub:641 + sub],
                                in_=Abuf[:, 512 + sub * 8:512 + (sub + 1) * 8],
                                axis=AX.X, op=A.add)
        st["vector"].append(msk)
        st = sch.stage()
        def rcp(e):
            e.activation(out=Abuf[:, 656:672], in_=Abuf[:, 640:656],
                         func=ACT.Ln, bias=cp.col(1e-20), scale=1.0)
            e.activation(out=Abuf[:, 672:688], in_=Abuf[:, 640:656],
                         func=ACT.Ln, bias=cp.col(1e-20), scale=1.0)
            e.activation(out=Abuf[:, 640:656], in_=Abuf[:, 656:672],
                         func=ACT.Exp, bias=cp.col(0.0), scale=-1.0)
        st["scalar"].append(rcp)
        st = sch.stage()
        def gnorm(e):
            for sub in range(16):
                e.tensor_scalar(out=Abuf[:, 512 + sub * 8:512 + (sub + 1) * 8],
                                in0=Abuf[:, 512 + sub * 8:512 + (sub + 1) * 8],
                                scalar1=Abuf[:, 640 + sub:641 + sub],
                                scalar2=None, op0=A.mult)
        st["vector"].append(gnorm)
        st = sch.stage()
        def trm(e):
            e.transpose(psT[:, 0:P], Abuf[:, 384:512], idn[:])
            e.transpose(psA[:, 0:P], Abuf[:, 512:640], idn[:])
        st["tensor"].append(trm)
        st = sch.stage()
        def evm(e):
            e.tensor_copy(out=Abuf[:, 1024:1152], in_=psT[:, 0:P])
            e.tensor_copy(out=Abuf[:, 1152:1280], in_=psA[:, 0:P])
        st["vector"].append(evm)
        st = sch.stage()
        def depart(e):
            for sub in range(16):
                e.dma_start(out=r32[0:8, sub * P:(sub + 1) * P],
                            in_=Abuf[sub * 8:(sub + 1) * 8, 1024:1152])
                e.dma_start(out=grows[0:8, sub * P:(sub + 1) * P],
                            in_=Abuf[sub * 8:(sub + 1) * 8, 1152:1280])
        st["scalar"].append(depart)
        st = sch.stage()
        def prep32(e):
            e.memset(ones32[:], 1.0)
        st["vector"].append(prep32)
        st = sch.stage()
        st["vector"].append(lambda e: e.tensor_tensor_scan(
            out=pos32[:], data0=ones32[:], data1=r32[:],
            initial=0.0, op0=A.mult, op1=A.add))
        st = sch.stage()
        st["sync"].append(lambda e: e.dma_start(out=ones32, in_=din["iota32"]))
        st = sch.stage()
        def slotrows(e):
            e.tensor_scalar(out=srows[:], in0=pos32[:],
                            scalar1=bas32[0:32, 0:1], scalar2=None, op0=A.add)
            e.tensor_scalar(out=s12[0:32, 2 * T:3 * T], in0=ones32,
                            scalar1=bas32[0:32, 1:2], scalar2=None, op0=A.add)
        st["vector"].append(slotrows)
        st = sch.stage()
        def slotrows2(e):
            e.tensor_tensor(out=srows[:], in0=srows[:],
                            in1=s12[0:32, 2 * T:3 * T], op=A.subtract)
            e.tensor_tensor(out=srows[:], in0=srows[:], in1=r32[:], op=A.mult)
            e.tensor_tensor(out=srows[:], in0=srows[:],
                            in1=s12[0:32, 2 * T:3 * T], op=A.add)
        st["vector"].append(slotrows2)
        for j in range(4):
            st = sch.stage()
            st["tensor"].append(lambda e, j=j: e.matmul(
                psA[0:32, 0:TSW], tril32[:, 0:32], r32[:, j * TSW:(j + 1) * TSW],
                start=True, stop=True))
            st = sch.stage()
            st["vector"].append(lambda e, j=j: e.tensor_copy(
                out=rnk[:, j * TSW:(j + 1) * TSW], in_=psA[0:32, 0:TSW]))
        st = sch.stage()
        def selk(e):
            for k in (1, 2):
                off = (k - 1) * T
                e.tensor_scalar(out=s12[0:32, off:off + T], in0=rnk[:],
                                scalar1=float(k), scalar2=None, op0=A.is_equal)
                e.tensor_tensor(out=s12[0:32, off:off + T],
                                in0=s12[0:32, off:off + T], in1=r32[:],
                                op=A.mult)
        st["vector"].append(selk)
        st = sch.stage()
        def gkslot(e):
            for k in (1, 2):
                off = (k - 1) * T
                e.tensor_tensor(out=gk[0:32, off:off + T],
                                in0=s12[0:32, off:off + T], in1=grows[:],
                                op=A.mult)
            for k in (1, 2):
                e.tensor_tensor(out=s12[0:32, (1 + k) * T:(2 + k) * T],
                                in0=s12[0:32, (k - 1) * T:k * T],
                                in1=srows[:], op=A.mult)
        st["vector"].append(gkslot)
        redsrc = [(gk, 0, 0), (gk, T, T), (s12, 2 * T, 2 * T), (s12, 3 * T, 3 * T)]
        for (srct, soff, doff) in redsrc:
            for j in range(4):
                st = sch.stage()
                st["tensor"].append(lambda e, srct=srct, soff=soff, j=j: e.matmul(
                    psA[:], ones32M[:],
                    srct[0:32, soff + j * TSW:soff + (j + 1) * TSW],
                    start=True, stop=True))
                st = sch.stage()
                st["vector"].append(lambda e, doff=doff, j=j: e.tensor_copy(
                    out=redbuf[:, doff + j * TSW:doff + (j + 1) * TSW],
                    in_=psA[:]))
        st = sch.stage()
        def r2c(e):
            for sub in range(16):
                e.dma_start(out=g12[:, sub:sub + 1],
                            in_=redbuf[0:1, sub * P:(sub + 1) * P])
                e.dma_start(out=g12[:, 16 + sub:17 + sub],
                            in_=redbuf[0:1, T + sub * P:T + (sub + 1) * P])
                e.dma_start(out=slotc[:, sub:sub + 1],
                            in_=redbuf[0:1, 2 * T + sub * P:2 * T + (sub + 1) * P])
                e.dma_start(out=slotc[:, 16 + sub:17 + sub],
                            in_=redbuf[0:1, 3 * T + sub * P:3 * T + (sub + 1) * P])
        st["scalar"].append(r2c)
        st = sch.stage()
        st["vector"].append(lambda e: e.tensor_copy(out=slotu[:], in_=slotc[:]))

        st = sch.stage()
        st["vector"].append(lambda e: e.memset(Abuf[:, 0:2048], 0.0))
        for z0 in range(0, NSLOT, P):
            st = sch.stage()
            st["gpsimd"].append(lambda e, z0=z0: e.dma_start(
                out=d_xg[z0:z0 + P, :], in_=Abuf[:, 0:2048]))
        for sub in range(16):
            st = sch.stage()
            st["sync"].append(lambda e, sub=sub: e.dma_start(
                out=Bbuf[:, 0:2048], in_=d_h1tm[sub * P:(sub + 1) * P, :]))
            st = sch.stage()
            def scale2(e, sub=sub):
                e.tensor_scalar(out=Cbuf[:, 0:2048], in0=Bbuf[:, 0:2048],
                                scalar1=g12[:, sub:sub + 1], scalar2=None,
                                op0=A.mult)
                e.tensor_scalar(out=Dbuf[:, 0:2048], in0=Bbuf[:, 0:2048],
                                scalar1=g12[:, 16 + sub:17 + sub], scalar2=None,
                                op0=A.mult)
            st["vector"].append(scale2)
            st = sch.stage()
            def scat(e, sub=sub):
                e.indirect_dma_start(
                    out=d_xg, out_offset=IndirectOffsetOnAxis(
                        ap=slotu[:, sub:sub + 1], axis=0),
                    in_=Cbuf[:, 0:2048], in_offset=None)
                e.indirect_dma_start(
                    out=d_xg, out_offset=IndirectOffsetOnAxis(
                        ap=slotu[:, 16 + sub:17 + sub], axis=0),
                    in_=Dbuf[:, 0:2048], in_offset=None)
            st["gpsimd"].append(scat)

        for e_ in range(E):
            for s0 in range(0, CAP, P):
                base = e_ * CAP + s0
                st = sch.stage()
                st["sync"].append(lambda e, base=base: e.dma_start(
                    out=Abuf[:, 0:2048], in_=d_xg[base:base + P, :]))
                qdq_emit(sch, Abuf[:, 0:2048], mm16, qscr, 2048, cp)
                for grp in range(4):
                    st = sch.stage()
                    def trq(e, grp=grp):
                        for j in range(4):
                            c = grp * 4 + j
                            e.transpose(pstb[:, j * P:(j + 1) * P],
                                        wq16[:, 4096 + c * P:4096 + (c + 1) * P], idnb[:])
                    st["tensor"].append(trq)
                    st = sch.stage()
                    st["vector"].append(lambda e, grp=grp: e.tensor_copy(
                        out=xqT16[:, grp * TSW:(grp + 1) * TSW],
                        in_=pstb[:, 0:TSW]))
                for jc in range(8):
                    st = sch.stage()
                    st["sync"].append(lambda e, e_=e_, jc=jc: e.dma_start(
                        out=wq16[:].rearrange("p (c t) -> p c t", t=TSW),
                        in_=d_w13qT[e_ * D:(e_ + 1) * D,
                                    jc * TSW:(jc + 1) * TSW]
                        .rearrange("(c p) t -> p c t", p=P)))
                    st = sch.stage()
                    def h13mm(e, jc=jc):
                        for c in range(NF):
                            e.matmul(pb[jc % 2][:],
                                     xqT16[:, c * P:(c + 1) * P],
                                     wq16[:, c * TSW:(c + 1) * TSW],
                                     start=(c == 0), stop=(c == NF - 1))
                    st["tensor"].append(h13mm)
                    st = sch.stage()
                    dstb = (Bbuf if jc < 4 else Cbuf)
                    st["vector"].append(lambda e, jc=jc, dstb=dstb: e.tensor_copy(
                        out=dstb[:, (jc % 4) * TSW:(jc % 4 + 1) * TSW],
                        in_=pb[jc % 2][:]))
                st = sch.stage()
                def hsil(e):
                    act512(e, Dbuf[:, 0:2048], Bbuf[:, 0:2048], ACT.Silu,
                           cp.col(0.0), 1.0, 2048)
                st["scalar"].append(hsil)
                st = sch.stage()
                def hmul(e):
                    e.tensor_tensor(out=Abuf[:, 0:2048], in0=Dbuf[:, 0:2048],
                                    in1=Cbuf[:, 0:2048], op=A.mult)
                st["vector"].append(hmul)
                qdq_emit(sch, Abuf[:, 0:2048], mm16, qscr, 2048, cp)
                for grp in range(4):
                    st = sch.stage()
                    def trh(e, grp=grp):
                        for j in range(4):
                            c = grp * 4 + j
                            e.transpose(pstb[:, j * P:(j + 1) * P],
                                        wq16[:, 4096 + c * P:4096 + (c + 1) * P], idnb[:])
                    st["tensor"].append(trh)
                    st = sch.stage()
                    st["vector"].append(lambda e, grp=grp: e.tensor_copy(
                        out=xqT16[:, grp * TSW:(grp + 1) * TSW],
                        in_=pstb[:, 0:TSW]))
                for jc in range(4):
                    st = sch.stage()
                    st["sync"].append(lambda e, e_=e_, jc=jc: e.dma_start(
                        out=wq16[:].rearrange("p (c t) -> p c t", t=TSW),
                        in_=d_w2qT[e_ * MOEH:(e_ + 1) * MOEH,
                                   jc * TSW:(jc + 1) * TSW]
                        .rearrange("(c p) t -> p c t", p=P)))
                    st = sch.stage()
                    def routmm(e, jc=jc):
                        for c in range(NF):
                            e.matmul(pb[jc % 2][:],
                                     xqT16[:, c * P:(c + 1) * P],
                                     wq16[:, c * TSW:(c + 1) * TSW],
                                     start=(c == 0), stop=(c == NF - 1))
                    st["tensor"].append(routmm)
                    st = sch.stage()
                    st["vector"].append(lambda e, jc=jc: e.tensor_copy(
                        out=Dbuf[:, jc * TSW:(jc + 1) * TSW], in_=pb[jc % 2][:]))
                st = sch.stage()
                st["gpsimd"].append(lambda e, base=base: e.dma_start(
                    out=d_rout[base:base + P, :], in_=Dbuf[:, 0:2048]))

        for ts in range(NTS):
            sl = slice(ts * TSW, (ts + 1) * TSW)
            for j in range(4):
                sub = ts * 4 + j
                st = sch.stage()
                def gath(e, sub=sub, j=j):
                    e.indirect_dma_start(
                        out=Cbuf[:, 0:2048], out_offset=None,
                        in_=d_rout, in_offset=IndirectOffsetOnAxis(
                            ap=slotu[:, sub:sub + 1], axis=0))
                    e.indirect_dma_start(
                        out=Dbuf[:, 0:2048], out_offset=None,
                        in_=d_rout, in_offset=IndirectOffsetOnAxis(
                            ap=slotu[:, 16 + sub:17 + sub], axis=0))
                st["gpsimd"].append(gath)
                st = sch.stage()
                def addr2(e, j=j):
                    e.tensor_tensor(out=Cbuf[:, 0:2048], in0=Cbuf[:, 0:2048],
                                    in1=Dbuf[:, 0:2048], op=A.add)
                st["vector"].append(addr2)
                for grp in range(4):
                    st = sch.stage()
                    def trr(e, grp=grp):
                        for jj in range(4):
                            c = grp * 4 + jj
                            e.transpose(psT[:, jj * P:(jj + 1) * P],
                                        Cbuf[:, c * P:(c + 1) * P], idn[:])
                    st["tensor"].append(trr)
                    st = sch.stage()
                    def evr(e, grp=grp, j=j):
                        for jj in range(4):
                            c = grp * 4 + jj
                            e.tensor_copy(
                                out=Rbuf[:, c * TSW + j * P:c * TSW + (j + 1) * P],
                                in_=psT[:, jj * P:(jj + 1) * P])
                    st["vector"].append(evr)
            st = sch.stage()
            st["sync"].append(lambda e, sl=sl: e.dma_start(
                out=_sb3(Abuf[:], TSW), in_=_rearr_fm(d_x2p[:, sl])))
            st = sch.stage()
            def x2f(e):
                for c in range(NF):
                    e.tensor_tensor(out=Abuf[:, c * TSW:(c + 1) * TSW],
                                    in0=Abuf[:, c * TSW:(c + 1) * TSW],
                                    in1=Rbuf[:, c * TSW:(c + 1) * TSW], op=A.add)
            st["vector"].append(x2f)
            st = sch.stage()
            st["gpsimd"].append(lambda e, sl=sl: e.dma_start(
                out=_rearr_fm(dout["x2"][:, sl]), in_=_sb3(Abuf[:], TSW)))
            ln_emit(sch, Abuf, Bbuf, Cbuf, psA, psB, onesM, TSW, bca, bcb, cp)
            st = sch.stage()
            st["sync"].append(lambda e: e.dma_start(
                out=wbuf[:, 0:NF * HD].rearrange("p (c t) -> p c t", t=HD),
                in_=_rearr_fm(din["wk1"])))
            st["sync"].append(lambda e: e.dma_start(
                out=Dbuf[:, 4096:4096 + NF * HD].rearrange("p (c t) -> p c t", t=HD),
                in_=_rearr_fm(din["wv1"])))
            st["sync"].append(lambda e: e.dma_start(
                out=Cbuf[:, 0:NF * P].rearrange("p (c t) -> p c t", t=P),
                in_=_rearr_fm(din["wcmp"])))
            st = sch.stage()
            def kvm2(e):
                for c in range(NF):
                    e.matmul(psA[0:HD, 0:TSW], wbuf[:, c * HD:(c + 1) * HD],
                             Bbuf[:, c * TSW:(c + 1) * TSW],
                             start=(c == 0), stop=(c == NF - 1))
                for c in range(NF):
                    e.matmul(psB[0:HD, 0:TSW],
                             Dbuf[:, 4096 + c * HD:4096 + (c + 1) * HD],
                             Bbuf[:, c * TSW:(c + 1) * TSW],
                             start=(c == 0), stop=(c == NF - 1))
                for c in range(NF):
                    e.matmul(psT[:, 0:TSW], Cbuf[:, c * P:(c + 1) * P],
                             Bbuf[:, c * TSW:(c + 1) * TSW],
                             start=(c == 0), stop=(c == NF - 1))
            st["tensor"].append(kvm2)
            st = sch.stage()
            def kvev2(e):
                e.tensor_copy(out=Dbuf[0:HD, 0:TSW], in_=psA[0:HD, 0:TSW])
                e.tensor_copy(out=Dbuf[0:HD, TSW:2 * TSW], in_=psB[0:HD, 0:TSW])
                e.tensor_copy(out=Dbuf[:, 2 * TSW:3 * TSW], in_=psT[:, 0:TSW])
            st["vector"].append(kvev2)
            st = sch.stage()
            def kvb(e):
                e.tensor_scalar(out=Dbuf[0:HD, 0:TSW], in0=Dbuf[0:HD, 0:TSW],
                                scalar1=kb1c[0:HD], scalar2=None, op0=A.add)
                e.tensor_scalar(out=Dbuf[0:HD, TSW:2 * TSW],
                                in0=Dbuf[0:HD, TSW:2 * TSW],
                                scalar1=vb1c[0:HD], scalar2=None, op0=A.add)
                e.tensor_scalar(out=Dbuf[:, 2 * TSW:3 * TSW],
                                in0=Dbuf[:, 2 * TSW:3 * TSW],
                                scalar1=cb1c[:], scalar2=None, op0=A.add)
            st["vector"].append(kvb)
            st = sch.stage()
            st["gpsimd"].append(lambda e, sl=sl: e.dma_start(
                out=dout["kraw"][:, sl], in_=Dbuf[0:HD, 0:TSW]))
            st["gpsimd"].append(lambda e, sl=sl: e.dma_start(
                out=dout["vraw"][:, sl], in_=Dbuf[0:HD, TSW:2 * TSW]))
            st["gpsimd"].append(lambda e, sl=sl: e.dma_start(
                out=dout["cmplog"][:, sl], in_=Dbuf[0:1, 2 * TSW:3 * TSW]))
        sch.emit(block)
    return nc


def build_launch3(max_stages=None):
    nc = bass.Bass("TRN2", target_bir_lowering=False, detect_race_conditions=False)
    din = {}
    for nm, shp, dt in [
            ("x2", [D, T], F32), ("x2b", [D, P], F32),
            ("krawf", [HD, SEQL], F32), ("vrawf", [HD, SEQL], F32),
            ("cmplogf", [P, 64], F32),
            ("wq1", [D, D], F32), ("bq1", [P, NF], F32),
            ("wlk", [D, D], F32), ("blk", [P, NF], F32),
            ("wlv", [D, D], F32), ("blv", [P, NF], F32),
            ("wo1", [D, D], F32), ("bo1", [P, NF], F32),
            ("qnw", [P, 1], F32), ("knw", [P, 1], F32),
            ("lcos", [P, P + T], F32), ("lsin", [P, P + T], F32),
            ("ccos", [P, CSEQ], F32), ("csin", [P, CSEQ], F32),
            ("swp", [P, P], F32),
            ("climit", [P, NF], F32),
            ("lmask", [P, 2 * 2 * P], F32),
            ("cmat", [P, CSEQ], F32),
            ("iden", [P, P], F32),
            ("gate_pk", [MLP, D // 2], I32), ("gate_sc", [MLP, 64], F32),
            ("up_pk", [MLP, D // 2], I32), ("up_sc", [MLP, 64], F32),
            ("down_pk", [D, MLP // 2], I32), ("down_sc", [D, MLP // 32], F32)]:
        din[nm] = nc.dram_tensor(nm, shp, dt, kind="ExternalInput").ap()
    import os as _os
    _k3 = "ExternalOutput" if _os.environ.get("L3DBG") else "Internal"
    d_out = nc.dram_tensor("out", [T, D], F32, kind="ExternalOutput").ap()
    d_x3 = nc.dram_tensor("x3", [D, T], F32, kind=_k3).ap()
    d_h2 = nc.dram_tensor("h2s", [D, T + P], F32, kind=_k3).ap()
    d_WgT = nc.dram_tensor("WgT", [D, MLP], F32, kind=_k3).ap()
    d_WuT = nc.dram_tensor("WuT", [D, MLP], F32, kind="Internal").ap()
    d_WdT = nc.dram_tensor("WdT", [MLP, D], F32, kind="Internal").ap()
    d_mrg = nc.dram_tensor("mrg", [D, T], F32, kind=_k3).ap()
    d_dbg3 = nc.dram_tensor("dbg3", [6 * P, 2048], F32, kind=_k3).ap()

    es = ExitStack()
    with es:
        sch = Sched(nc, es)
        cp = ConstPool(nc, es, sch)
        def sb(name, shape, dt=F32):
            return es.enter_context(nc.sbuf_tensor(name, shape, dt))
        NW = NF * TSW
        Abuf = sb("Abuf", [P, NW]); Bbuf = sb("Bbuf", [P, NW])
        Cbuf = sb("Cbuf", [P, NW]); Dbuf = sb("Dbuf", [P, NW])
        ints = sb("ints", [P, 2048], I32)
        wbuf = sb("wbuf", [P, NF * P])
        onesM = sb("onesM", [P, P]); idn = sb("idn", [P, P])
        qnwc = sb("qnwc", [P, 1]); knwc = sb("knwc", [P, 1])
        swp = sb("swp_sb", [P, P])
        lcos = sb("lcos_sb", [P, P + T]); lsin = sb("lsin_sb", [P, P + T])
        climit = sb("climit_sb", [P, NF]); lmask = sb("lmask_sb", [P, 4 * P])
        ckf = sb("ckf", [P, CSEQ]); cvt = sb("cvt", [P, CSEQ])
        qh = sb("qh", [P, T])
        lkh = sb("lkh", [P, P + T]); lvt = sb("lvt", [P, P + T])

        bq1c = sb("bq1c", [P, NF]); blkc = sb("blkc", [P, NF])
        blvc = sb("blvc", [P, NF]); bo1c = sb("bo1c", [P, NF])
        bca = Cbuf[:, 6144:6144 + TSW]
        bcb = Cbuf[:, 6656:6656 + TSW]
        ccos = Dbuf[:, 4096:4096 + CSEQ]
        csin = Dbuf[:, 6144:6144 + CSEQ]
        pb = [es.enter_context(nc.psum_tensor(f"pb{i}", [P, TSW], F32)) for i in range(2)]
        psA = es.enter_context(nc.psum_tensor("psA", [P, TSW], F32))
        psB = es.enter_context(nc.psum_tensor("psB", [P, TSW], F32))
        psT = es.enter_context(nc.psum_tensor("psT", [P, TSW], F32))
        psC = es.enter_context(nc.psum_tensor("psC", [P, TSW], F32))
        block = es.enter_context(nc.Block())

        st = sch.stage()
        for t_, src in [(idn, din["iden"]), (qnwc, din["qnw"]), (knwc, din["knw"]),
                        (swp, din["swp"]), (lcos, din["lcos"]), (lsin, din["lsin"]),
                        (ccos, din["ccos"]), (csin, din["csin"]),
                        (climit, din["climit"]), (lmask, din["lmask"]),
                        (bq1c, din["bq1"]),
                        (blkc, din["blk"]), (blvc, din["blv"]),
                        (bo1c, din["bo1"])]:
            st["sync"].append(lambda e, t_=t_, src=src: e.dma_start(out=t_[:], in_=src))
        st["vector"].append(lambda e: e.memset(onesM[:], 1.0))

        def nv_decode(dpk, dsc, nrows, kcols, ddstT):
            kk = kcols // 2
            npass = max(kk // 1024, 1)
            kw = 2048
            for n0 in range(0, nrows, P):
                for pi in range(npass):
                    st = sch.stage()
                    st["sync"].append(lambda e, n0=n0, pi=pi: e.dma_start(
                        out=ints[:, 0:1024],
                        in_=dpk[n0:n0 + P, pi * 1024:(pi + 1) * 1024]))
                    st["sync"].append(lambda e, n0=n0, pi=pi: e.dma_start(
                        out=bca[:, 0:64], in_=dsc[n0:n0 + P, pi * 64:(pi + 1) * 64]))
                    st = sch.stage()
                    def ibits2(e):
                        kh = 1024
                        dint = Dbuf[:, 0:kh].bitcast(I32)
                        e.tensor_scalar(out=dint, in0=ints[:, 0:kh], scalar1=4,
                                        scalar2=None, op0=A.logical_shift_right)
                        e.tensor_scalar(out=dint, in0=dint, scalar1=15,
                                        scalar2=None, op0=A.bitwise_and)
                        e.tensor_scalar(out=ints[:, 0:kh], in0=ints[:, 0:kh],
                                        scalar1=15, scalar2=None, op0=A.bitwise_and)
                        e.tensor_copy(out=Abuf[:, 0:kw:2], in_=ints[:, 0:kh])
                        e.tensor_copy(out=Abuf[:, 1:kw:2], in_=dint)
                    st["vector"].append(ibits2)
                    st = sch.stage()
                    def dval(e):
                        e.tensor_scalar(out=Bbuf[:, 0:kw], in0=Abuf[:, 0:kw],
                                        scalar1=8.0, scalar2=None, op0=A.is_ge)
                        e.scalar_tensor_tensor(out=Cbuf[:, 0:kw], in0=Bbuf[:, 0:kw],
                                               scalar=-8.0, in1=Abuf[:, 0:kw],
                                               op0=A.mult, op1=A.add)
                        e.tensor_scalar(out=Dbuf[:, 0:kw], in0=Cbuf[:, 0:kw],
                                        scalar1=0.5, scalar2=None, op0=A.mult)
                        e.tensor_scalar(out=Dbuf[:, 0:kw], in0=Dbuf[:, 0:kw],
                                        scalar1=-0.25, scalar2=None, op0=A.add)
                        e.tensor_copy(out=ints[:, 0:kw], in_=Dbuf[:, 0:kw])
                        e.tensor_copy(out=Abuf[:, 0:kw], in_=ints[:, 0:kw])
                        e.scalar_tensor_tensor(out=Dbuf[:, 0:kw], in0=Abuf[:, 0:kw],
                                               scalar=-2.0, in1=Cbuf[:, 0:kw],
                                               op0=A.mult, op1=A.add)
                        e.tensor_scalar(out=Cbuf[:, 0:kw], in0=Abuf[:, 0:kw],
                                        scalar1=2.0, scalar2=None, op0=A.is_ge)
                        e.scalar_tensor_tensor(out=Abuf[:, 0:kw], in0=Cbuf[:, 0:kw],
                                               scalar=-2.0, in1=Abuf[:, 0:kw],
                                               op0=A.mult, op1=A.add)
                        e.tensor_scalar(out=Cbuf[:, 0:kw], in0=Cbuf[:, 0:kw],
                                        scalar1=3.0, scalar2=None, op0=A.mult)
                        e.tensor_scalar(out=Cbuf[:, 0:kw], in0=Cbuf[:, 0:kw],
                                        scalar1=1.0, scalar2=None, op0=A.add)
                        e.tensor_scalar(out=Abuf[:, 0:kw], in0=Abuf[:, 0:kw],
                                        scalar1=1.0, scalar2=None, op0=A.add)
                        e.tensor_tensor(out=Cbuf[:, 0:kw], in0=Cbuf[:, 0:kw],
                                        in1=Abuf[:, 0:kw], op=A.mult)
                        e.tensor_scalar(out=Abuf[:, 0:kw], in0=Dbuf[:, 0:kw],
                                        scalar1=0.25, scalar2=None, op0=A.mult)
                        e.tensor_scalar(out=Abuf[:, 0:kw], in0=Abuf[:, 0:kw],
                                        scalar1=0.5, scalar2=None, op0=A.add)
                        e.tensor_tensor(out=Abuf[:, 0:kw], in0=Abuf[:, 0:kw],
                                        in1=Cbuf[:, 0:kw], op=A.mult)
                    st["vector"].append(dval)
                    st = sch.stage()
                    def dval2(e):
                        e.tensor_copy(out=Cbuf[:, 0:kw], in_=ints[:, 0:kw])
                        e.tensor_scalar(out=Dbuf[:, 0:kw], in0=Dbuf[:, 0:kw],
                                        scalar1=0.5, scalar2=None, op0=A.mult)
                        e.tensor_tensor(out=Cbuf[:, 0:kw], in0=Cbuf[:, 0:kw],
                                        in1=Dbuf[:, 0:kw], op=A.add)
                        e.tensor_copy(out=Dbuf[:, 0:kw], in_=ints[:, 0:kw])
                        e.tensor_scalar(out=Dbuf[:, 0:kw], in0=Dbuf[:, 0:kw],
                                        scalar1=1.5, scalar2=None, op0=A.is_le)
                        e.tensor_tensor(out=Cbuf[:, 0:kw], in0=Cbuf[:, 0:kw],
                                        in1=Abuf[:, 0:kw], op=A.subtract)
                        e.tensor_tensor(out=Cbuf[:, 0:kw], in0=Cbuf[:, 0:kw],
                                        in1=Dbuf[:, 0:kw], op=A.mult)
                        e.tensor_tensor(out=Cbuf[:, 0:kw], in0=Cbuf[:, 0:kw],
                                        in1=Abuf[:, 0:kw], op=A.add)
                        e.tensor_scalar(out=Bbuf[:, 0:kw], in0=Bbuf[:, 0:kw],
                                        scalar1=-2.0, scalar2=None, op0=A.mult)
                        e.tensor_scalar(out=Bbuf[:, 0:kw], in0=Bbuf[:, 0:kw],
                                        scalar1=1.0, scalar2=None, op0=A.add)
                        e.tensor_tensor(out=Cbuf[:, 0:kw], in0=Cbuf[:, 0:kw],
                                        in1=Bbuf[:, 0:kw], op=A.mult)
                        e.tensor_tensor(
                            out=Cbuf[:, 0:kw].rearrange("p (b k) -> p b k", k=32),
                            in0=Cbuf[:, 0:kw].rearrange("p (b k) -> p b k", k=32),
                            in1=bca[:, 0:64].broadcast_to([P, 64, 32]),
                            op=A.mult)
                    st["vector"].append(dval2)
                    for grp in range(4):
                        st = sch.stage()
                        def trg(e, grp=grp):
                            for j in range(4):
                                c = grp * 4 + j
                                e.transpose(psT[:, j * P:(j + 1) * P],
                                            Cbuf[:, c * P:(c + 1) * P], idn[:])
                        st["tensor"].append(trg)
                        st = sch.stage()
                        st["vector"].append(lambda e, grp=grp: e.tensor_copy(
                            out=Dbuf[:, grp * TSW:(grp + 1) * TSW], in_=psT[:]))
                    st = sch.stage()
                    st["gpsimd"].append(lambda e, n0=n0, pi=pi: e.dma_start(
                        out=ddstT[pi * kw:(pi + 1) * kw, n0:n0 + P]
                        .rearrange("(c p) t -> p c t", p=P),
                        in_=Dbuf[:, 0:kw].rearrange("p (c t) -> p c t", t=P)))
        nv_decode(din["gate_pk"], din["gate_sc"], MLP, D, d_WgT)
        nv_decode(din["up_pk"], din["up_sc"], MLP, D, d_WuT)
        nv_decode(din["down_pk"], din["down_sc"], D, MLP, d_WdT)

        for ts in range(NTS):
            sl = slice(ts * TSW, (ts + 1) * TSW)
            st = sch.stage()
            st["sync"].append(lambda e, sl=sl: e.dma_start(
                out=_sb3(Abuf[:], TSW), in_=_rearr_fm(din["x2"][:, sl])))
            ln_emit(sch, Abuf, Bbuf, Cbuf, psA, psB, onesM, TSW, bca, bcb, cp)
            st = sch.stage()
            st["gpsimd"].append(lambda e, sl=sl: e.dma_start(
                out=_rearr_fm(d_h2[:, sl]), in_=_sb3(Bbuf[:], TSW)))
        st = sch.stage()
        st["sync"].append(lambda e: e.dma_start(
            out=_sb3(Abuf[:, 0:NF * P], P), in_=_rearr_fm(din["x2b"])))
        ln_emit(sch, Abuf[:, 0:NF * P], Bbuf[:, 0:NF * P], Cbuf[:, 0:NF * P],
                psA, psB, onesM, P, bca[:, 0:P], bcb[:, 0:P], cp)
        st = sch.stage()
        st["gpsimd"].append(lambda e: e.dma_start(
            out=_rearr_fm(d_h2[:, T:T + P]), in_=_sb3(Bbuf[:, 0:NF * P], P)))

        d_wrow = nc.dram_tensor("wrow", [1, SEQL], F32, kind="Internal").ap()
        st = sch.stage()
        st["sync"].append(lambda e: e.dma_start(out=Dbuf[:, 0:64],
                                                in_=din["cmplogf"]))
        st = sch.stage()
        def csm(e):
            lg3 = Dbuf[:, 0:64].rearrange("p (g r) -> p g r", r=4)
            e.tensor_reduce(out=Dbuf[:, 64:80], in_=lg3, axis=AX.X, op=A.max)
            e.tensor_copy(out=Dbuf[:, 112:113], in_=Dbuf[:, 0:1])
            e.tensor_copy(out=Dbuf[:, 113:114], in_=Dbuf[:, 0:1])
            e.tensor_tensor(out=Dbuf[:, 128:192].rearrange("p (g r) -> p g r", r=4),
                            in0=lg3,
                            in1=Dbuf[:, 64:80].broadcast_to([P, 16, 4]),
                            op=A.subtract)
        st["vector"].append(csm)
        st = sch.stage()
        st["scalar"].append(lambda e: e.activation(
            out=Dbuf[:, 128:192], in_=Dbuf[:, 128:192], func=ACT.Exp,
            bias=cp.col(0.0), scale=1.0))
        st = sch.stage()
        def csum(e):
            e.tensor_reduce(out=Dbuf[:, 64:80],
                            in_=Dbuf[:, 128:192].rearrange("p (g r) -> p g r", r=4),
                            axis=AX.X, op=A.add)
        st["vector"].append(csum)
        st = sch.stage()
        def crc(e):
            e.activation(out=Dbuf[:, 80:96], in_=Dbuf[:, 64:80], func=ACT.Ln,
                         bias=cp.col(0.0), scale=1.0)
            e.activation(out=Dbuf[:, 96:112], in_=Dbuf[:, 64:80], func=ACT.Ln,
                         bias=cp.col(0.0), scale=1.0)
            e.activation(out=Dbuf[:, 64:80], in_=Dbuf[:, 80:96], func=ACT.Exp,
                         bias=cp.col(0.0), scale=-1.0)
        st["scalar"].append(crc)
        st = sch.stage()
        def cwn(e):
            e.tensor_tensor(out=Dbuf[:, 128:192].rearrange("p (g r) -> p g r", r=4),
                            in0=Dbuf[:, 128:192].rearrange("p (g r) -> p g r", r=4),
                            in1=Dbuf[:, 64:80].broadcast_to([P, 16, 4]),
                            op=A.mult)
        st["vector"].append(cwn)
        st = sch.stage()
        st["gpsimd"].append(lambda e: e.dma_start(out=d_wrow,
                                                  in_=Dbuf[:, 128:192]))
        st = sch.stage()
        st["sync"].append(lambda e: e.dma_start(out=Bbuf[0:1, 0:SEQL],
                                                in_=d_wrow))
        def bcast_row(src_row):
            for j in range(SEQL // TSW):
                st = sch.stage()
                st["tensor"].append(lambda e, j=j: e.matmul(
                    psA[:], onesM[0:1, :], src_row[0:1, j * TSW:(j + 1) * TSW],
                    start=True, stop=True))
                st = sch.stage()
                st["vector"].append(lambda e, j=j: e.tensor_copy(
                    out=Cbuf[:, j * TSW:(j + 1) * TSW], in_=psA[:]))
        bcast_row(Bbuf)
        st = sch.stage()
        st["sync"].append(lambda e: e.dma_start(out=Abuf[0:HD, 0:SEQL],
                                                in_=din["krawf"]))
        st = sch.stage()
        st["vector"].append(lambda e: e.tensor_tensor(
            out=Abuf[0:HD, 0:SEQL], in0=Abuf[0:HD, 0:SEQL],
            in1=Cbuf[0:HD, 0:SEQL], op=A.mult))
        st = sch.stage()
        st["vector"].append(lambda e: e.tensor_reduce(
            out=ckf[0:HD, 0:CSEQ],
            in_=Abuf[0:HD, 0:SEQL].rearrange("p (c r) -> p c r", r=4),
            axis=AX.X, op=A.add))
        st = sch.stage()
        def cksq(e):
            for j in range(4):
                e.activation(out=Bbuf[:, j * TSW:(j + 1) * TSW],
                             in_=ckf[:, j * TSW:(j + 1) * TSW],
                             func=ACT.Square, bias=cp.col(0.0), scale=1.0)
        st["scalar"].append(cksq)
        for j in range(4):
            st = sch.stage()
            st["tensor"].append(lambda e, j=j: e.matmul(
                psA[:], onesM[:], Bbuf[:, j * TSW:(j + 1) * TSW],
                start=True, stop=True))
            st = sch.stage()
            st["vector"].append(lambda e, j=j: e.tensor_scalar(
                out=Cbuf[:, j * TSW:(j + 1) * TSW],
                in0=psA[:], scalar1=1.0 / HD, scalar2=None, op0=A.mult))
        st = sch.stage()
        def cksqr(e):
            for j in range(4):
                sl2 = slice(j * TSW, (j + 1) * TSW)
                e.activation(out=Dbuf[:, sl2], in_=Cbuf[:, sl2], func=ACT.Sqrt,
                             bias=cp.col(1e-6), scale=1.0)
                e.activation(out=Bbuf[:, sl2], in_=Dbuf[:, sl2], func=ACT.Ln,
                             bias=cp.col(0.0), scale=1.0)
                e.activation(out=Cbuf[:, sl2], in_=Bbuf[:, sl2], func=ACT.Exp,
                             bias=cp.col(0.0), scale=-1.0)
        st["scalar"].append(cksqr)
        st = sch.stage()
        def ckap(e):
            e.tensor_tensor(out=ckf[:, 0:CSEQ], in0=ckf[:, 0:CSEQ],
                            in1=Cbuf[:, 0:CSEQ], op=A.mult)
            e.tensor_scalar(out=ckf[:, 0:CSEQ], in0=ckf[:, 0:CSEQ],
                            scalar1=knwc[:], scalar2=None, op0=A.mult)
        st["vector"].append(ckap)
        for j in range(4):
            st = sch.stage()
            st["tensor"].append(lambda e, j=j: e.matmul(
                psA[:], swp[:], ckf[:, j * TSW:(j + 1) * TSW],
                start=True, stop=True))
            st = sch.stage()
            def ckrope(e, j=j):
                sl2 = slice(j * TSW, (j + 1) * TSW)
                e.tensor_tensor(out=Bbuf[:, sl2], in0=ckf[:, sl2],
                                in1=ccos[:, sl2], op=A.mult)
                e.tensor_tensor(out=Cbuf[:, sl2], in0=psA[:], in1=csin[:, sl2],
                                op=A.mult)
                e.tensor_tensor(out=ckf[:, sl2], in0=Bbuf[:, sl2],
                                in1=Cbuf[:, sl2], op=A.add)
            st["vector"].append(ckrope)
        st = sch.stage()
        st["sync"].append(lambda e: e.dma_start(out=Abuf[0:HD, 0:SEQL],
                                                in_=din["vrawf"]))
        st = sch.stage()
        st["vector"].append(lambda e: e.tensor_tensor(
            out=Abuf[0:HD, 0:SEQL], in0=Abuf[0:HD, 0:SEQL],
            in1=Cbuf[0:HD, 0:SEQL] if False else Cbuf[0:HD, 0:SEQL], op=A.mult))
        sch.stages.pop()
        st = sch.stage()
        st["sync"].append(lambda e: e.dma_start(out=Bbuf[0:1, 0:SEQL],
                                                in_=d_wrow))
        bcast_row(Bbuf)
        st = sch.stage()
        st["vector"].append(lambda e: e.tensor_tensor(
            out=Abuf[0:HD, 0:SEQL], in0=Abuf[0:HD, 0:SEQL],
            in1=Cbuf[0:HD, 0:SEQL], op=A.mult))
        st = sch.stage()
        st["vector"].append(lambda e: e.tensor_reduce(
            out=Bbuf[0:HD, 0:CSEQ],
            in_=Abuf[0:HD, 0:SEQL].rearrange("p (c r) -> p c r", r=4),
            axis=AX.X, op=A.add))
        for grp in range(4):
            st = sch.stage()
            def trcv(e, grp=grp):
                for j in range(4):
                    c = grp * 4 + j
                    e.transpose(psT[:, j * P:(j + 1) * P],
                                Bbuf[0:HD, c * P:(c + 1) * P], idn[:])
            st["tensor"].append(trcv)
            st = sch.stage()
            st["vector"].append(lambda e, grp=grp: e.tensor_copy(
                out=cvt[:, grp * TSW:(grp + 1) * TSW], in_=psT[:]))

        def headproj(dw, bias_c, h, out_t, ncols, with_boundary):
            hsl = slice(h * P, (h + 1) * P)
            st = sch.stage()
            st["sync"].append(lambda e, hsl=hsl, dw=dw: e.dma_start(
                out=_sb3(wbuf[:], P), in_=_rearr_fm(dw[:, hsl])))
            off = 0
            if with_boundary:
                st = sch.stage()
                st["sync"].append(lambda e: e.dma_start(
                    out=_sb3(Abuf[:, 0:NF * P], P),
                    in_=_rearr_fm(d_h2[:, T:T + P])))
                st = sch.stage()
                def bmm(e):
                    for c in range(NF):
                        e.matmul(pb[0][:, 0:P], wbuf[:, c * P:(c + 1) * P],
                                 Abuf[:, c * P:(c + 1) * P],
                                 start=(c == 0), stop=(c == NF - 1))
                st["tensor"].append(bmm)
                st = sch.stage()
                st["scalar"].append(lambda e: e.copy(out=out_t[:, 0:P],
                                                     in_=pb[0][:, 0:P]))
                off = P
            for ts in range(NTS):
                sl = slice(ts * TSW, (ts + 1) * TSW)
                st = sch.stage()
                st["sync"].append(lambda e, sl=sl: e.dma_start(
                    out=_sb3(Abuf[:], TSW), in_=_rearr_fm(d_h2[:, sl])))
                st = sch.stage()
                def qmm(e):
                    for c in range(NF):
                        e.matmul(pb[0][:], wbuf[:, c * P:(c + 1) * P],
                                 Abuf[:, c * TSW:(c + 1) * TSW],
                                 start=(c == 0), stop=(c == NF - 1))
                st["tensor"].append(qmm)
                st = sch.stage()
                st["scalar"].append(lambda e, ts=ts, off=off: e.copy(
                    out=out_t[:, off + ts * TSW:off + (ts + 1) * TSW],
                    in_=pb[0][:]))
            st = sch.stage()
            st["vector"].append(lambda e, h=h, off=off, ncols=ncols: e.tensor_scalar(
                out=out_t[:, 0:ncols], in0=out_t[:, 0:ncols],
                scalar1=bias_c[:, h // 1:h // 1 + 1] if False else bias_c[:, h:h + 1],
                scalar2=None, op0=A.add))

        def rms_rope(out_t, ncols, nw_col, tcos, tsin, do_rope=True):
            nts2 = ncols // TSW + (1 if ncols % TSW else 0)
            st = sch.stage()
            def sq(e):
                for j in range(nts2):
                    w2_ = min(TSW, ncols - j * TSW)
                    e.activation(out=Bbuf[:, j * TSW:j * TSW + w2_],
                                 in_=out_t[:, j * TSW:j * TSW + w2_],
                                 func=ACT.Square, bias=cp.col(0.0), scale=1.0)
            st["scalar"].append(sq)
            for j in range(nts2):
                w2_ = min(TSW, ncols - j * TSW)
                st = sch.stage()
                st["tensor"].append(lambda e, j=j, w2_=w2_: e.matmul(
                    psA[:, 0:w2_], onesM[:], Bbuf[:, j * TSW:j * TSW + w2_],
                    start=True, stop=True))
                st = sch.stage()
                st["vector"].append(lambda e, j=j, w2_=w2_: e.tensor_scalar(
                    out=Cbuf[:, j * TSW:j * TSW + w2_], in0=psA[:, 0:w2_],
                    scalar1=1.0 / HD, scalar2=None, op0=A.mult))
            st = sch.stage()
            def sqr(e):
                for j in range(nts2):
                    w2_ = min(TSW, ncols - j * TSW)
                    sl2 = slice(j * TSW, j * TSW + w2_)
                    e.activation(out=Dbuf[:, sl2], in_=Cbuf[:, sl2],
                                 func=ACT.Sqrt, bias=cp.col(1e-6), scale=1.0)
                    e.activation(out=Bbuf[:, sl2], in_=Dbuf[:, sl2],
                                 func=ACT.Ln, bias=cp.col(0.0), scale=1.0)
                    e.activation(out=Cbuf[:, sl2], in_=Bbuf[:, sl2],
                                 func=ACT.Exp, bias=cp.col(0.0), scale=-1.0)
            st["scalar"].append(sqr)
            st = sch.stage()
            def ap_(e):
                e.tensor_tensor(out=out_t[:, 0:ncols], in0=out_t[:, 0:ncols],
                                in1=Cbuf[:, 0:ncols], op=A.mult)
                e.tensor_scalar(out=out_t[:, 0:ncols], in0=out_t[:, 0:ncols],
                                scalar1=nw_col[:], scalar2=None, op0=A.mult)
            st["vector"].append(ap_)
            if not do_rope:
                return
            for j in range(nts2):
                w2_ = min(TSW, ncols - j * TSW)
                st = sch.stage()
                st["tensor"].append(lambda e, j=j, w2_=w2_: e.matmul(
                    psA[:, 0:w2_], swp[:], out_t[:, j * TSW:j * TSW + w2_],
                    start=True, stop=True))
                st = sch.stage()
                def rope(e, j=j, w2_=w2_):
                    sl2 = slice(j * TSW, j * TSW + w2_)
                    e.tensor_tensor(out=Bbuf[:, sl2], in0=out_t[:, sl2],
                                    in1=tcos[:, sl2], op=A.mult)
                    e.tensor_tensor(out=Cbuf[:, sl2], in0=psA[:, 0:w2_],
                                    in1=tsin[:, sl2], op=A.mult)
                    e.tensor_tensor(out=out_t[:, sl2], in0=Bbuf[:, sl2],
                                    in1=Cbuf[:, sl2], op=A.add)
                st["vector"].append(rope)

        for h in range(NH):
            headproj(din["wq1"], bq1c, h, qh, T, False)
            rms_rope(qh, T, qnwc, lcos[:, P:], lsin[:, P:])
            st = sch.stage()
            st["vector"].append(lambda e: e.tensor_scalar(
                out=qh[:], in0=qh[:], scalar1=ATTN_SCALE, scalar2=None,
                op0=A.mult))
            headproj(din["wlk"], blkc, h, lkh, P + T, True)
            rms_rope(lkh, P + T, knwc, lcos, lsin)
            headproj(din["wlv"], blvc, h, lvt, P + T, True)
            for grp in range(5):
                st = sch.stage()
                def trlv(e, grp=grp):
                    for j in range(4):
                        c = grp * 4 + j
                        if c >= 17:
                            break
                        e.transpose(psT[:, j * P:(j + 1) * P],
                                    lvt[:, c * P:(c + 1) * P], idn[:])
                st["tensor"].append(trlv)
                st = sch.stage()
                def evlv(e, grp=grp):
                    for j in range(4):
                        c = grp * 4 + j
                        if c >= 17:
                            break
                        e.tensor_copy(out=Dbuf[:, c * P:(c + 1) * P],
                                      in_=psT[:, j * P:(j + 1) * P])
                st["vector"].append(evlv)
            st = sch.stage()
            st["vector"].append(lambda e: e.tensor_copy(out=lvt[:, 0:17 * P],
                                                        in_=Dbuf[:, 0:17 * P]))
            st = sch.stage()
            st["sync"].append(lambda e: e.dma_start(out=wbuf[:], in_=din["cmat"]))
            for qt in range(NF):
                qsl = slice(qt * P, (qt + 1) * P)
                for j in range(4):
                    st = sch.stage()
                    st["tensor"].append(lambda e, j=j, qsl=qsl: e.matmul(
                        psA[:], qh[:, qsl], ckf[:, j * TSW:(j + 1) * TSW],
                        start=True, stop=True))
                    st = sch.stage()
                    st["scalar"].append(lambda e, j=j: e.copy(
                        out=Abuf[:, j * TSW:(j + 1) * TSW], in_=psA[:]))
                if h == 0 and qt == 0 and True:
                    st = sch.stage()
                    st["gpsimd"].append(lambda e: e.dma_start(
                        out=d_dbg3[0:P, :], in_=Abuf[:, 0:2048]))
                st = sch.stage()
                def amask(e, qt=qt):
                    e.tensor_scalar(out=Bbuf[:, 0:2048], in0=wbuf[:],
                                    scalar1=climit[:, qt:qt + 1], scalar2=None,
                                    op0=A.is_gt)
                    e.scalar_tensor_tensor(out=Abuf[:, 0:2048],
                                           in0=Bbuf[:, 0:2048], scalar=NEG,
                                           in1=Abuf[:, 0:2048],
                                           op0=A.mult, op1=A.add)
                st["vector"].append(amask)
                if h == 0 and qt == 0:
                    st = sch.stage()
                    st["gpsimd"].append(lambda e: e.dma_start(
                        out=d_dbg3[P:2 * P, :], in_=Abuf[:, 0:2048]))
                st = sch.stage()
                def tops(e):
                    e.reduce_max(out=Bbuf[:, 0:1], in_=Abuf[:, 0:2048],
                                 axis=AX.X, negate=True)
                    e.max(out=Bbuf[:, 8:16], in_=Abuf[:, 0:2048])
                st["vector"].append(tops)
                st = sch.stage()
                def expn(e):
                    for j in range(4):
                        e.activation(out=Cbuf[:, j * TSW:(j + 1) * TSW],
                                     in_=Abuf[:, j * TSW:(j + 1) * TSW],
                                     func=ACT.Exp, bias=Bbuf[:, 0:1], scale=1.0)
                st["scalar"].append(expn)
                st = sch.stage()
                def dmask(e):
                    e.tensor_scalar(out=Dbuf[:, 0:2048], in0=Abuf[:, 0:2048],
                                    scalar1=Bbuf[:, 15:16], scalar2=None,
                                    op0=A.is_ge)
                    e.tensor_tensor(out=Cbuf[:, 0:2048], in0=Cbuf[:, 0:2048],
                                    in1=Dbuf[:, 0:2048], op=A.mult)
                    e.tensor_reduce(out=Bbuf[:, 1:2], in_=Cbuf[:, 0:2048],
                                    axis=AX.X, op=A.add)
                    e.tensor_scalar(out=Bbuf[:, 3:4], in0=Bbuf[:, 0:1],
                                    scalar1=5e29, scalar2=None, op0=A.is_lt)
                    e.tensor_copy(out=Bbuf[:, 7:8], in_=Bbuf[:, 0:1])
                    e.tensor_scalar(out=Bbuf[:, 2:3], in0=Bbuf[:, 1:2],
                                    scalar1=1e-9, scalar2=None, op0=A.max)
                st["vector"].append(dmask)
                st = sch.stage()
                def drcp(e):
                    e.activation(out=Bbuf[:, 4:5], in_=Bbuf[:, 2:3],
                                 func=ACT.Ln, bias=cp.col(0.0), scale=1.0)
                    e.activation(out=Bbuf[:, 6:7], in_=Bbuf[:, 2:3],
                                 func=ACT.Ln, bias=cp.col(0.0), scale=1.0)
                    e.activation(out=Bbuf[:, 5:6], in_=Bbuf[:, 4:5],
                                 func=ACT.Exp, bias=cp.col(0.0), scale=-1.0)
                st["scalar"].append(drcp)
                st = sch.stage()
                def dsc(e):
                    e.tensor_tensor(out=Bbuf[:, 5:6], in0=Bbuf[:, 5:6],
                                    in1=Bbuf[:, 3:4], op=A.mult)
                    e.tensor_copy(out=Bbuf[:, 7:8], in_=Bbuf[:, 0:1])
                    e.tensor_copy(out=Bbuf[:, 6:7], in_=Bbuf[:, 0:1])
                    e.tensor_scalar(out=Cbuf[:, 0:2048], in0=Cbuf[:, 0:2048],
                                    scalar1=Bbuf[:, 5:6], scalar2=None,
                                    op0=A.mult)
                st["vector"].append(dsc)
                if h == 0 and qt == 0:
                    st = sch.stage()
                    st["gpsimd"].append(lambda e: e.dma_start(
                        out=d_dbg3[2 * P:3 * P, :], in_=Cbuf[:, 0:2048]))
                    st["gpsimd"].append(lambda e: e.dma_start(
                        out=d_dbg3[3 * P:4 * P, 0:16], in_=Bbuf[:, 0:16]))
                for grp in range(4):
                    st = sch.stage()
                    def trd(e, grp=grp):
                        for j in range(4):
                            c = grp * 4 + j
                            e.transpose(psT[:, j * P:(j + 1) * P],
                                        Cbuf[:, c * P:(c + 1) * P], idn[:])
                    st["tensor"].append(trd)
                    st = sch.stage()
                    st["vector"].append(lambda e, grp=grp: e.tensor_copy(
                        out=Dbuf[:, grp * TSW:(grp + 1) * TSW], in_=psT[:]))
                st = sch.stage()
                def ctxs(e):
                    for c in range(NF):
                        e.matmul(psC[:, 0:P], Dbuf[:, c * P:(c + 1) * P],
                                 cvt[:, c * P:(c + 1) * P],
                                 start=(c == 0), stop=False)
                st["tensor"].append(ctxs)
                st = sch.stage()
                def lsc(e, qt=qt, qsl=qsl):
                    e.matmul(psB[:, 0:2 * P], qh[:, qsl],
                             lkh[:, qt * P:qt * P + 2 * P],
                             start=True, stop=True)
                st["tensor"].append(lsc)
                st = sch.stage()
                def lmsk(e, qt=qt):
                    moff = 0 if qt == 0 else 2 * P
                    e.tensor_tensor(out=Abuf[:, 0:2 * P], in0=psB[:, 0:2 * P],
                                    in1=lmask[:, moff:moff + 2 * P], op=A.add)
                    e.tensor_copy(out=Bbuf[:, 7:8], in_=lmask[:, 0:1])
                    e.tensor_copy(out=Bbuf[:, 6:7], in_=lmask[:, 0:1])
                    e.reduce_max(out=Bbuf[:, 0:1], in_=Abuf[:, 0:2 * P],
                                 axis=AX.X, negate=True)
                st["vector"].append(lmsk)
                st = sch.stage()
                st["scalar"].append(lambda e: e.activation(
                    out=Abuf[:, 512:512 + 2 * P], in_=Abuf[:, 0:2 * P],
                    func=ACT.Exp, bias=Bbuf[:, 0:1], scale=1.0,
                    accum_out=Bbuf[:, 1:2]))
                st = sch.stage()
                def lrcp(e):
                    e.activation(out=Bbuf[:, 4:5], in_=Bbuf[:, 1:2],
                                 func=ACT.Ln, bias=cp.col(0.0), scale=1.0)
                    e.activation(out=Bbuf[:, 6:7], in_=Bbuf[:, 1:2],
                                 func=ACT.Ln, bias=cp.col(0.0), scale=1.0)
                    e.activation(out=Bbuf[:, 5:6], in_=Bbuf[:, 4:5],
                                 func=ACT.Exp, bias=cp.col(0.0), scale=-1.0)
                st["scalar"].append(lrcp)
                st = sch.stage()
                st["vector"].append(lambda e: e.tensor_scalar(
                    out=Abuf[:, 512:512 + 2 * P], in0=Abuf[:, 512:512 + 2 * P],
                    scalar1=Bbuf[:, 5:6], scalar2=None, op0=A.mult))
                st = sch.stage()
                def trpl(e):
                    e.transpose(psT[:, 0:P], Abuf[:, 512:512 + P], idn[:])
                    e.transpose(psT[:, P:2 * P], Abuf[:, 512 + P:512 + 2 * P],
                                idn[:])
                st["tensor"].append(trpl)
                st = sch.stage()
                st["vector"].append(lambda e: e.tensor_copy(
                    out=Abuf[:, 1024:1024 + 2 * P], in_=psT[:, 0:2 * P]))
                st = sch.stage()
                def ctxl(e, qt=qt):
                    e.matmul(psC[:, 0:P], Abuf[:, 1024:1024 + P],
                             lvt[:, qt * P:(qt + 1) * P],
                             start=False, stop=False)
                    e.matmul(psC[:, 0:P], Abuf[:, 1024 + P:1024 + 2 * P],
                             lvt[:, (qt + 1) * P:(qt + 2) * P],
                             start=False, stop=True)
                st["tensor"].append(ctxl)
                st = sch.stage()
                st["scalar"].append(lambda e: e.activation(
                    out=Abuf[:, 0:P], in_=psC[:, 0:P], func=ACT.Copy,
                    bias=0.0, scale=0.5))
                st = sch.stage()
                st["tensor"].append(lambda e: e.transpose(
                    psT[:, 0:P], Abuf[:, 0:P], idn[:]))
                st = sch.stage()
                st["vector"].append(lambda e: e.tensor_copy(
                    out=Abuf[:, P:2 * P], in_=psT[:, 0:P]))
                st = sch.stage()
                st["gpsimd"].append(lambda e, h=h, qt=qt: e.dma_start(
                    out=d_mrg[h * P:(h + 1) * P, qt * P:(qt + 1) * P],
                    in_=Abuf[:, P:2 * P]))

        for ts in range(NTS):
            sl = slice(ts * TSW, (ts + 1) * TSW)
            st = sch.stage()
            st["sync"].append(lambda e, sl=sl: e.dma_start(
                out=_sb3(Abuf[:], TSW), in_=_rearr_fm(d_mrg[:, sl])))
            proj_emit(sch, nc, Abuf, TSW, Cbuf, din["wo1"], wbuf, bo1c, pb, cp)
            st = sch.stage()
            st["sync"].append(lambda e, sl=sl: e.dma_start(
                out=_sb3(Abuf[:], TSW), in_=_rearr_fm(din["x2"][:, sl])))
            st = sch.stage()
            def x3f(e):
                for c in range(NF):
                    e.tensor_tensor(out=Abuf[:, c * TSW:(c + 1) * TSW],
                                    in0=Abuf[:, c * TSW:(c + 1) * TSW],
                                    in1=Cbuf[:, c * TSW:(c + 1) * TSW], op=A.add)
            st["vector"].append(x3f)
            st = sch.stage()
            st["gpsimd"].append(lambda e, sl=sl: e.dma_start(
                out=_rearr_fm(d_x3[:, sl]), in_=_sb3(Abuf[:], TSW)))
            ln_emit(sch, Abuf, Bbuf, Cbuf, psA, psB, onesM, TSW, bca, bcb, cp)
            st = sch.stage()
            st["vector"].append(lambda e: e.memset(Dbuf[:], 0.0))
            for nch in range(MLP // TSW):
                nsl = slice(nch * TSW, (nch + 1) * TSW)
                st = sch.stage()
                st["sync"].append(lambda e, nsl=nsl: e.dma_start(
                    out=_sb3(wbuf[:], P) if False else wbuf[:].rearrange("p (c t) -> p c t", t=P),
                    in_=d_WgT[:, nsl].rearrange("(c p) t -> p c t", p=P)
                    if False else d_WgT[:, ncha_dummy] if False else d_WgT[:, nsl].rearrange("(c p) n -> p c n", p=P)[:, :, 0:P] if False else d_WgT[:, nch * TSW:nch * TSW + P].rearrange("(c p) t -> p c t", p=P)))
                sch.stages.pop()
                for nsub in range(4):
                    ncol = nch * TSW + nsub * P
                    st = sch.stage()
                    st["sync"].append(lambda e, ncol=ncol: e.dma_start(
                        out=wbuf[:].rearrange("p (c t) -> p c t", t=P),
                        in_=d_WgT[:, ncol:ncol + P].rearrange("(c p) t -> p c t", p=P)))
                    st = sch.stage()
                    def gmm(e):
                        for c in range(NF):
                            e.matmul(pb[0][:], wbuf[:, c * P:(c + 1) * P],
                                     Bbuf[:, c * TSW:(c + 1) * TSW],
                                     start=(c == 0), stop=(c == NF - 1))
                    st["tensor"].append(gmm)
                    st = sch.stage()
                    st["scalar"].append(lambda e, nsub=nsub: e.activation(
                        out=Cbuf[:, nsub * TSW:nsub * TSW + TSW], in_=pb[0][:],
                        func=ACT.Gelu_apprx_tanh, bias=cp.col(0.0), scale=1.0))
                    st = sch.stage()
                    st["sync"].append(lambda e, ncol=ncol: e.dma_start(
                        out=wbuf[:].rearrange("p (c t) -> p c t", t=P),
                        in_=d_WuT[:, ncol:ncol + P].rearrange("(c p) t -> p c t", p=P)))
                    st = sch.stage()
                    def umm(e):
                        for c in range(NF):
                            e.matmul(pb[1][:], wbuf[:, c * P:(c + 1) * P],
                                     Bbuf[:, c * TSW:(c + 1) * TSW],
                                     start=(c == 0), stop=(c == NF - 1))
                    st["tensor"].append(umm)
                    st = sch.stage()
                    def inter(e, nsub=nsub):
                        e.tensor_tensor(out=Cbuf[:, nsub * TSW:(nsub + 1) * TSW],
                                        in0=Cbuf[:, nsub * TSW:(nsub + 1) * TSW],
                                        in1=pb[1][:], op=A.mult)
                    st["vector"].append(inter)
                for dc in range(NF):
                    st = sch.stage()
                    st["sync"].append(lambda e, nsl=nsl, dc=dc: e.dma_start(
                        out=wbuf[:, 0:4 * P].rearrange("p (c t) -> p c t", t=P),
                        in_=d_WdT[nsl, dc * P:(dc + 1) * P]
                        .rearrange("(c p) t -> p c t", p=P)))
                    st = sch.stage()
                    def dmm(e, dc=dc):
                        for c in range(4):
                            e.matmul(psA[:], wbuf[:, c * P:(c + 1) * P],
                                     Cbuf[:, c * TSW:(c + 1) * TSW],
                                     start=(c == 0), stop=(c == 3))
                    st["tensor"].append(dmm)
                    st = sch.stage()
                    def dacc(e, dc=dc):
                        e.tensor_tensor(out=Dbuf[:, dc * TSW:(dc + 1) * TSW],
                                        in0=Dbuf[:, dc * TSW:(dc + 1) * TSW],
                                        in1=psA[:], op=A.add)
                    st["vector"].append(dacc)
            st = sch.stage()
            st["sync"].append(lambda e, sl=sl: e.dma_start(
                out=_sb3(Abuf[:], TSW), in_=_rearr_fm(d_x3[:, sl])))
            st = sch.stage()
            def outf(e):
                for c in range(NF):
                    e.tensor_tensor(out=Abuf[:, c * TSW:(c + 1) * TSW],
                                    in0=Abuf[:, c * TSW:(c + 1) * TSW],
                                    in1=Dbuf[:, c * TSW:(c + 1) * TSW], op=A.add)
            st["vector"].append(outf)
            for sub in range(4):
                for grp in range(4):
                    st = sch.stage()
                    def tro(e, sub=sub, grp=grp):
                        for j in range(4):
                            c = grp * 4 + j
                            e.transpose(psT[:, j * P:(j + 1) * P],
                                        Abuf[:, c * TSW + sub * P:c * TSW + (sub + 1) * P],
                                        idn[:])
                    st["tensor"].append(tro)
                    st = sch.stage()
                    st["vector"].append(lambda e, grp=grp: e.tensor_copy(
                        out=Cbuf[:, grp * TSW:(grp + 1) * TSW], in_=psT[:]))
                st = sch.stage()
                st["gpsimd"].append(lambda e, ts=ts, sub=sub: e.dma_start(
                    out=d_out[(ts * 4 + sub) * P:(ts * 4 + sub + 1) * P, :],
                    in_=Cbuf[:, 0:4 * TSW]))
        sch.emit(block, max_stages)
    return nc

def _l3_consts(core):
    seq, pos = divmod(core, 4)
    own0 = pos * T
    half = 32
    inv = (1.0 / 10000.0 ** (np.arange(half) * (2.0 / 64.0)))
    def tables(positions):
        ang = positions[None, :] * inv[:, None]
        c = np.cos(ang); s_ = np.sin(ang)
        n = positions.shape[0]
        tc = np.ones((P, n)); ts_ = np.zeros((P, n))
        tc[0:32] = c; tc[32:64] = c
        ts_[0:32] = -s_; ts_[32:64] = s_
        return tc.astype(np.float32), ts_.astype(np.float32)
    lpos = np.arange(own0 - P, own0 + T).astype(np.float64)
    lcos, lsin = tables(lpos)
    bends = (np.arange(1, CSEQ + 1) * RATIO - 1).astype(np.float64)
    ccos, csin = tables(bends)
    qabs = own0 + np.arange(T)
    clim = np.floor((qabs - 3) / 4.0).astype(np.float32)
    climit = np.ascontiguousarray(clim.reshape(NF, P).T)
    cmat = np.tile(np.arange(CSEQ, dtype=np.float32)[None, :], (P, 1))
    i = np.arange(P)[:, None]
    j = np.arange(2 * P)[None, :]
    rel = i + P - j
    band = (rel >= 0) & (rel <= WIN)
    m1 = np.where(band, 0.0, NEG).astype(np.float32)
    key_abs0 = own0 - P + j
    m0 = np.where(band & (key_abs0 >= 0), 0.0, NEG).astype(np.float32)
    lmask = np.concatenate([m0, m1], axis=1).astype(np.float32)
    swp = np.zeros((P, P), np.float32)
    for d in range(32):
        swp[d + 32, d] = 1.0
        swp[d, d + 32] = 1.0
    return dict(lcos=np.ascontiguousarray(lcos), lsin=np.ascontiguousarray(lsin),
                ccos=np.ascontiguousarray(ccos), csin=np.ascontiguousarray(csin),
                climit=climit, cmat=cmat, lmask=lmask, swp=swp)


def kernel(**inputs):
    f32 = lambda k: np.asarray(inputs[k], np.float32)
    x = f32("L0_attn_x")
    iden = np.eye(P, dtype=np.float32)
    nc1 = build_launch1()
    nc2 = build_launch2()
    nc3 = build_launch3()
    cp1 = nc1._cpool.host_array()
    cp2 = nc2._cpool.host_array()
    cp3 = nc3._cpool.host_array()

    wkT = np.ascontiguousarray(f32("L0_attn_Wk").T)
    wvT = np.ascontiguousarray(f32("L0_attn_Wv").T)
    bkc = _bias_cols(f32("L0_attn_bk")); bvc = _bias_cols(f32("L0_attn_bv"))
    maps1 = []
    for core in range(NCORE):
        seq, pos = divmod(core, 4)
        xc = x[seq, pos * T:(pos + 1) * T]
        maps1.append({"x": _fmT(xc), "wkT": wkT, "wvT": wvT,
                      "bk": bkc, "bv": bvc, "cpool": cp1})
    res1 = run_bass_kernel_spmd(nc1, maps1, core_ids=list(range(NCORE)))
    scar = [r["scar"] for r in res1.results]
    zcum = [r["zcum"] for r in res1.results]

    iota32 = np.zeros((32, T), np.float32)
    iota32[:8] = np.arange(T, dtype=np.float32)[None, :]
    tril32 = np.zeros((32, 32), np.float32)
    for i in range(8):
        tril32[:i + 1, i] = 1.0
    bas32 = np.zeros((32, 2), np.float32)
    for e in range(8):
        bas32[e, 0] = e * CAP - 1
        bas32[e, 1] = NSLOT + e * T
    com2 = {
        "wqT": np.ascontiguousarray(f32("L0_attn_Wq").T),
        "bq": _bias_cols(f32("L0_attn_bq")),
        "woT": np.ascontiguousarray(f32("L0_attn_Wo").T),
        "bo": _bias_cols(f32("L0_attn_bo")),
        "gwT": np.ascontiguousarray(f32("L0_ffn_gate_w").T),
        "ebias": np.tile(f32("L0_ffn_expert_bias")[None, :], (P, 1)),
        "sw1T": np.ascontiguousarray(f32("L0_ffn_sw1").T),
        "sw3T": np.ascontiguousarray(f32("L0_ffn_sw3").T),
        "sw2T": np.ascontiguousarray(f32("L0_ffn_sw2").T),
        "w13": np.ascontiguousarray(f32("L0_ffn_w13").reshape(E * 2 * MOEH, D)),
        "w2": np.ascontiguousarray(f32("L0_ffn_w2").reshape(E * D, MOEH)),
        "wk1": np.ascontiguousarray(f32("L1_attn_Wk")),
        "bk1": np.ascontiguousarray(f32("L1_attn_bk").reshape(P, 1)),
        "wv1": np.ascontiguousarray(f32("L1_attn_Wv")),
        "bv1": np.ascontiguousarray(f32("L1_attn_bv").reshape(P, 1)),
        "wcmp": np.ascontiguousarray(np.tile(f32("L1_attn_Wcmp"), (1, P))),
        "bcmp": np.full((P, 1), float(f32("L1_attn_bcmp")[0]), np.float32),
        "iden": iden, "iota32": iota32, "tril32": tril32, "bas32": bas32,
        "cpool": cp2,
    }
    maps2 = []
    for core in range(NCORE):
        seq, pos = divmod(core, 4)
        xc = x[seq, pos * T:(pos + 1) * T]
        carr = np.zeros((P, 3 * NF), np.float32)
        for j in range(pos):
            carr[:, j * NF:(j + 1) * NF] = scar[seq * 4 + j]
        m = dict(com2)
        m["x"] = _fmT(xc)
        m["zcum"] = zcum[core]
        m["carries"] = carr
        maps2.append(m)
    res2 = run_bass_kernel_spmd(nc2, maps2, core_ids=list(range(NCORE)))
    x2 = [r["x2"] for r in res2.results]
    kraw = [r["kraw"] for r in res2.results]
    vraw = [r["vraw"] for r in res2.results]
    cmplog = [r["cmplog"] for r in res2.results]

    com3 = {
        "wq1": np.ascontiguousarray(f32("L1_attn_Wq")),
        "bq1": _bias_cols(f32("L1_attn_bq")),
        "wlk": np.ascontiguousarray(f32("L1_attn_Wlk")),
        "blk": _bias_cols(f32("L1_attn_blk")),
        "wlv": np.ascontiguousarray(f32("L1_attn_Wlv")),
        "blv": _bias_cols(f32("L1_attn_blv")),
        "wo1": np.ascontiguousarray(f32("L1_attn_Wo")),
        "bo1": _bias_cols(f32("L1_attn_bo")),
        "qnw": np.ascontiguousarray(f32("L1_attn_qn_w").reshape(P, 1)),
        "knw": np.ascontiguousarray(f32("L1_attn_kn_w").reshape(P, 1)),
        "iden": iden,
        "gate_pk": np.asarray(inputs["L1_ffn_gate_W_packed"], np.int32),
        "gate_sc": f32("L1_ffn_gate_scales"),
        "up_pk": np.asarray(inputs["L1_ffn_up_W_packed"], np.int32),
        "up_sc": f32("L1_ffn_up_scales"),
        "down_pk": np.asarray(inputs["L1_ffn_down_W_packed"], np.int32),
        "down_sc": f32("L1_ffn_down_scales"),
        "cpool": cp3,
    }
    maps3 = []
    for core in range(NCORE):
        seq, pos = divmod(core, 4)
        m = dict(com3)
        m.update(_l3_consts(core))
        m["x2"] = x2[core]
        if pos == 0:
            m["x2b"] = np.zeros((D, P), np.float32)
        else:
            m["x2b"] = np.ascontiguousarray(x2[core - 1][:, T - P:])
        m["krawf"] = np.ascontiguousarray(
            np.concatenate([kraw[seq * 4 + j] for j in range(4)], axis=1))
        m["vrawf"] = np.ascontiguousarray(
            np.concatenate([vraw[seq * 4 + j] for j in range(4)], axis=1))
        cl = np.concatenate([cmplog[seq * 4 + j][0] for j in range(4)])
        m["cmplogf"] = np.ascontiguousarray(cl.reshape(P, 64))
        maps3.append(m)
    res3 = run_bass_kernel_spmd(nc3, maps3, core_ids=list(range(NCORE)))
    outs = [r["out"] for r in res3.results]
    full = np.zeros((B, SEQL, D), np.float32)
    for core in range(NCORE):
        seq, pos = divmod(core, 4)
        full[seq, pos * T:(pos + 1) * T] = outs[core]
    return full
